# version 11
# baseline (speedup 1.0000x reference)
import dataclasses
from contextlib import ExitStack

import numpy as np
import ml_dtypes

import concourse.bass as bass
import concourse.mybir as mybir
from concourse.bass_utils import run_bass_kernel_spmd

F32 = mybir.dt.float32
BF16 = mybir.dt.bfloat16
AF = mybir.ActivationFunctionType
ALU = mybir.AluOpType
AX = mybir.AxisListType

B = 4
S = 4096
E = 2048
H = 16
DH = 128
HID = 5461
TO = 2048
NB_K = 32
NB_Q = 16
KTs = [8, 16, 24, 32]
NSLOT = 4
QT = 512
NW2 = float(HID * E)
SCL = float(np.float32(DH ** -0.5))
EXP_SHIFT = -25.0
EPS = 1e-5
INV_E = float(np.float32(1.0 / E))
TH_SCALE = float(np.float32(0.5 / NW2))
NHC = 43
HC_LAST = HID - 42 * 128

ENGS = ("sync", "gpsimd", "vector", "scalar", "tensor")
import os as _os
MAX_PHASE = int(_os.environ.get("BASSK_MAX_PHASE", "9"))
P3SUB = int(_os.environ.get("BASSK_P3SUB", "9"))
LNMODE = int(_os.environ.get("BASSK_LNMODE", "4"))


def _r(ap, pattern, **kw):
    return ap.rearrange(pattern, **kw)


def bcast_mid(ap2d, rep):
    return dataclasses.replace(ap2d, ap=[ap2d.ap[0], [0, rep], ap2d.ap[1]])


def half_view(ap2d, nh, half):
    return dataclasses.replace(
        ap2d,
        offset=ap2d.offset + half * 64,
        ap=[ap2d.ap[0], [128, nh], [1, 64]],
    )


def half_bcast(ap2d, nh, half):
    return dataclasses.replace(
        ap2d,
        offset=ap2d.offset + half * 64,
        ap=[ap2d.ap[0], [0, nh], [1, 64]],
    )



def dma3(b, qeng, sem, out3, in3, parts):
    mid = out3.shape[1]
    assert in3.shape[1] == mid
    step = (mid + parts - 1) // parts
    c = 0
    for s0 in range(0, mid, step):
        s1 = min(mid, s0 + step)
        c = b.dma(qeng, sem, out3[:, s0:s1, :], in3[:, s0:s1, :])
    return c


class Carve:
    def __init__(self, tensor, total):
        self.t = tensor
        self.off = 0
        self.total = total

    def take(self, cols):
        assert self.off + cols <= self.total, (self.off, cols, self.total)
        ap = self.t[:, self.off:self.off + cols]
        self.off += cols
        return ap


class Bld:

    def __init__(self, nc, eng, engobj, sems):
        self.nc = nc
        self.eng = eng
        self.engobj = engobj
        self.sems = sems
        self.cnt = {k: 0 for k in sems}

        class _Null:
            def __getattr__(self, name):
                return lambda *a, **k: None

        null = _Null()
        self.S = engobj if eng == "sync" else null
        self.G = engobj if eng == "gpsimd" else null
        self.V = engobj if eng == "vector" else null
        self.A = engobj if eng == "scalar" else null
        self.T = engobj if eng == "tensor" else null

    def on(self, e):
        return self.eng == e

    def sig(self, inst, sem, n=1):
        self.cnt[sem] += n
        if inst is not None:
            inst.then_inc(self.sems[sem], n)
        return self.cnt[sem]

    def w(self, eng, sem, val):
        if val <= 0:
            return
        assert val <= self.cnt[sem], (sem, val, self.cnt[sem])
        if self.eng == eng:
            self.engobj.wait_ge(self.sems[sem], val)

    def dma(self, qeng, sem, out, in_):
        inst = None
        if self.eng == qeng:
            inst = self.engobj.dma_start(out=out, in_=in_)
        return self.sig(inst, sem, 16)

    def barrier(self, state):
        for sem in ["ld", "st"] + [f"d{i}" for i in range(8)] + [f"s{i}" for i in range(4)]:
            self.w("sync", sem, self.cnt[sem])
        for sem in ["sd", "act"] + [f"s{i}" for i in range(4)]:
            self.w("scalar", sem, self.cnt[sem])
        self.w("gpsimd", "gd", self.cnt["gd"])
        self.w("gpsimd", "gq", self.cnt["gq"])
        self.w("vector", "dve", self.cnt["dve"])
        self.w("tensor", "pe", self.cnt["pe"])
        for e in ENGS:
            inst = self.engobj.nop() if self.eng == e else None
            self.sig(inst, "ph", 1)
        tgt = self.cnt["ph"]
        self.engobj.wait_ge(self.sems["ph"], tgt)



def emit_program(b, t):
    nc = b.nc
    stats = t["stats"]
    cpool = t["cpool"]
    pa, pb = t["pa"], t["pb"]
    BFA, FAA = t["BFA"], t["FAA"]
    bst = {}

    for val, idx in t["consts"].items():
        b.V.memset(cpool[:, idx:idx + 1], val)
    b.V.memset(t["ones_b"][:], 1.0)
    b.V.memset(t["onesr_f"][:], 1.0)
    i = b.V.memset(t["ones_f"][:], 1.0)
    b.sig(i, "dve")
    b.dma("sync", "ld", t["identsb"][:], t["ident"])
    b.dma("sync", "ld", t["biascsb"][:], t["biasc"])
    b.barrier(bst)
    if MAX_PHASE < 1:
        return

    bf = Carve(BFA, t["BF_COLS"])
    fa = Carve(FAA, t["FA_COLS"])
    wsb = bf.take(16 * 2048)
    xnb = [bf.take(2048), bf.take(2048)]
    xhT = [bf.take(2048), bf.take(2048)]
    sqscr = bf.take(2048)
    qks = [bf.take(1024), bf.take(1024)]
    fmstg = [bf.take(2048), bf.take(2048)]
    vstg = [bf.take(2048), bf.take(2048)]
    xin = [fa.take(2048), fa.take(2048)]
    ctile = [fa.take(128), fa.take(128)]
    stile = [fa.take(128), fa.take(128)]
    t1f = [fa.take(1024), fa.take(1024)]
    t2f = [fa.take(1024), fa.take(1024)]
    wtl = [fa.take(2048), fa.take(2048)]
    spscr = bf.take(2048)

    wsb3 = _r(wsb, "p (ec d) -> p ec d", ec=16)

    sp_tiles = []
    for mat, acc in (("w1T", 25), ("w3T", 26)):
        for pc in range(16):
            for f0, fw in ((0, 2048), (2048, 2048), (4096, HID - 4096)):
                sp_tiles.append((mat, pc * 128, f0, fw, acc))
    for pc in range(NHC):
        hw = 128 if pc < 42 else HC_LAST
        sp_tiles.append(("w2T", pc * 128, 0, 2048, 27))
    sp_idx = [0]
    acc_init = {25: False, 26: False, 27: False}
    sp_done = {}
    sp_add = {}

    def emit_spass_tiles(n):
        for _ in range(n):
            if sp_idx[0] >= len(sp_tiles):
                return
            k = sp_idx[0]
            mat, r0, f0, fw, acc = sp_tiles[k]
            rows = 128 if mat != "w2T" else min(128, HID - r0)
            slot = k % 2
            b.w("sync", "act", sp_done.get(k - 2, 0))
            c_ld = b.dma("sync", f"d{4 + slot}", wtl[slot][0:rows, 0:fw],
                         t[mat][r0:r0 + rows, f0:f0 + fw])
            b.w("scalar", f"d{4 + slot}", c_ld)
            b.w("scalar", "gq", sp_add.get(k - 1, 0))
            i = b.A.activation(spscr[0:rows, 0:fw], wtl[slot][0:rows, 0:fw],
                               AF.Abs, accum_out=stats[0:rows, 28:29])
            c = b.sig(i, "act")
            sp_done[k] = c
            b.w("gpsimd", "act", c)
            if not acc_init[acc]:
                i = b.G.tensor_scalar(stats[0:rows, acc:acc + 1],
                                      stats[0:rows, 28:29], 1.0, None, ALU.mult)
                acc_init[acc] = True
            else:
                i = b.G.tensor_add(stats[0:rows, acc:acc + 1],
                                   stats[0:rows, acc:acc + 1],
                                   stats[0:rows, 28:29])
            c = b.sig(i, "gq")
            b.w("gpsimd", "gq", c)
            sp_add[k] = c
            sp_idx[0] += 1


    part_end_pe = 0
    ln_ring = [0, 8]
    pe_ptfree = {}
    qk_spill_done = {}
    v_spill_done = {}
    xhT_done = {}
    rope_done = {}
    vevac_done = {}
    qks_free = {}
    fmstg_free = {}
    vstg_free = {}
    xin_free = {}
    mm_done = {}
    xnb_cons = {}
    xhT_mm = {}

    PT = [pb[:, 0:512].bitcast(BF16), pb[:, 512:1024].bitcast(BF16)]

    def ln_normalize(xsrc, dst_bf, par, rows=128):
        sb = ln_ring[par % 2]
        c_sq_prev = None
        i = b.V.reduce_sum(stats[:, sb + 0:sb + 1], xsrc, axis=AX.X)
        c1 = b.sig(i, "dve")
        i = b.A.activation(sqscr[:], xsrc, AF.Square,
                           accum_out=stats[:, sb + 1:sb + 2])
        ca = b.sig(i, "act")
        b.w("vector", "dve", c1)
        b.w("vector", "act", ca)
        i = b.V.tensor_scalar(stats[:, sb + 2:sb + 3], stats[:, sb + 0:sb + 1],
                              INV_E, None, ALU.mult)
        c = b.sig(i, "dve")
        b.w("vector", "dve", c)
        i = b.V.scalar_tensor_tensor(stats[:, sb + 3:sb + 4],
                                     stats[:, sb + 2:sb + 3], -1.0,
                                     stats[:, sb + 2:sb + 3],
                                     ALU.mult, ALU.mult)
        c = b.sig(i, "dve")
        b.w("vector", "dve", c)
        i = b.V.tensor_scalar(stats[:, sb + 4:sb + 5], stats[:, sb + 1:sb + 2],
                              INV_E, stats[:, sb + 3:sb + 4],
                              ALU.mult, ALU.add)
        cvar = b.sig(i, "dve")
        b.w("scalar", "dve", cvar)
        i = b.A.activation(stats[:, sb + 5:sb + 6], stats[:, sb + 4:sb + 5],
                           AF.Ln, bias=EPS)
        c = b.sig(i, "act")
        b.w("scalar", "act", c)
        i = b.A.activation(stats[:, sb + 6:sb + 7], stats[:, sb + 5:sb + 6],
                           AF.Exp, scale=-0.5)
        crstd = b.sig(i, "act")
        b.w("vector", "act", crstd)
        i = b.V.tensor_scalar(dst_bf, xsrc, stats[:, sb + 2:sb + 3],
                              stats[:, sb + 6:sb + 7], ALU.subtract, ALU.mult)
        return b.sig(i, "dve")

    def xhat_transposes(par):
        b.w("vector", "pe", xhT_mm.get(par, 0))
        for g in range(2):
            b.w("tensor", "dve", pe_ptfree.get(g, 0))
            last = None
            for j8 in range(8):
                j = g * 8 + j8
                i = b.T.transpose(PT[g][:, j8 * 128:(j8 + 1) * 128],
                                  xnb[par][:, j * 128:(j + 1) * 128],
                                  t["identsb"][:])
                if i is not None:
                    last = i
            c = b.sig(last, "pe")
            b.w("vector", "pe", c)
            i = b.V.tensor_copy(xhT[par][:, g * 1024:(g + 1) * 1024], PT[g][:])
            pe_ptfree[g] = b.sig(i, "dve")
        xhT_done[par] = b.cnt["dve"]

    parts = [("q", 0, NB_Q), ("k", 2048, NB_K), ("v", 4096, NB_K)]
    for pidx, (part, poff, nblk) in enumerate(parts):
        b.w("gpsimd", "pe", part_end_pe)
        wld = 0
        for c4 in range(4):
            wld = dma3(b, "gpsimd", "gd",
                       wsb3[:, :, c4 * 512:(c4 + 1) * 512],
                       _r(t["wqkvT"][:, poff + c4 * 512: poff + (c4 + 1) * 512],
                          "(ec p) d -> p ec d", p=128), 2)
        if part == "q":
            xsrc_t, cos_d, sin_d = t["x_own"], t["cos_o"], t["sinA_o"]
        else:
            xsrc_t, cos_d, sin_d = t["x_tok"], t["cos_t"], t["sinA_t"]
        xld_map = {}
        ccs_map = {}
        cnorm_map = {}

        def p1_loads(tb):
            par = tb % 2
            dcnt, acnt = xin_free.get(par, (0, 0))
            b.w("sync", "dve", dcnt)
            b.w("sync", "act", acnt)
            xld_map[tb] = b.dma("sync", f"d{par}", xin[par][:],
                                xsrc_t[tb * 128:(tb + 1) * 128, :])
            if part in ("q", "k"):
                b.w("sync", "dve", rope_done.get((par, 0), 0))
                b.w("sync", "dve", rope_done.get((par, 1), 0))
                b.dma("sync", f"d{2 + par}", ctile[par][:],
                      cos_d[tb * 128:(tb + 1) * 128, :])
                ccs_map[par] = b.dma("sync", f"d{2 + par}", stile[par][:],
                                     sin_d[tb * 128:(tb + 1) * 128, :])

        def p1_stage_ln(tb):
            par = tb % 2
            b.w("vector", f"d{par}", xld_map[tb])
            b.w("scalar", f"d{par}", xld_map[tb])
            b.w("vector", "pe", xnb_cons.get(par, 0))
            cnorm = ln_normalize(xin[par][:], xnb[par][:], tb)
            xin_free[par] = (cnorm, b.cnt["act"])
            cnorm_map[tb] = cnorm

        def p1_stage_tp(tb):
            par = tb % 2
            b.w("tensor", "dve", cnorm_map[tb])
            xhat_transposes(par)
            xnb_cons[par] = b.cnt["pe"]

        p1_loads(0)
        p1_stage_ln(0)
        p1_stage_tp(0)
        for tb in range(nblk):
            par = tb % 2
            if tb + 1 < nblk:
                p1_loads(tb + 1)
            c_cs = ccs_map.get(par, 0)
            for dh in range(2):
                b.w("tensor", "act", vevac_done.get((dh,), 0))
                b.w("tensor", "dve", rope_done.get(("t1", dh), 0))
                b.w("tensor", "gd", wld)
                b.w("tensor", "dve", xhT_done[par])
                last = None
                for ec in range(16):
                    i = b.T.matmul(pa[:, dh * 1024: dh * 1024 + 512],
                                   xhT[par][:, ec * 128:(ec + 1) * 128],
                                   wsb3[:, ec, dh * 1024: dh * 1024 + 512],
                                   start=(ec == 0), stop=(ec == 15))
                    i2 = b.T.matmul(pa[:, dh * 1024 + 512: dh * 1024 + 1024],
                                    xhT[par][:, ec * 128:(ec + 1) * 128],
                                    wsb3[:, ec, dh * 1024 + 512: dh * 1024 + 1024],
                                    start=(ec == 0), stop=(ec == 15))
                    last = i2 if i2 is not None else i
                c_mm = b.sig(last, "pe")
                mm_done[(par, dh)] = c_mm
                paH = pa[:, dh * 1024:(dh + 1) * 1024]
                if part in ("q", "k"):
                    b.w("vector", "pe", c_mm)
                    b.w("vector", f"d{2 + par}", c_cs)
                    pa3 = _r(paH, "p (h d) -> p h d", h=8)
                    t13 = _r(t1f[dh][:], "p (h d) -> p h d", h=8)
                    cosb = bcast_mid(ctile[par][:], 8)
                    i = b.V.tensor_tensor(t13, pa3, cosb, ALU.mult)
                    b.sig(i, "dve")
                    for hf in range(2):
                        ov = half_view(t2f[dh][:], 8, hf)
                        iv = half_view(paH, 8, 1 - hf)
                        sv = half_bcast(stile[par][:], 8, hf)
                        i = b.V.tensor_tensor(ov, iv, sv, ALU.mult)
                        b.sig(i, "dve")
                    b.w("vector", "dve", b.cnt["dve"])
                    b.w("vector", "pe", qks_free.get(dh, 0))
                    i = b.V.tensor_add(qks[dh][:], t1f[dh][:], t2f[dh][:])
                    c = b.sig(i, "dve")
                    rope_done[("t1", dh)] = c
                    rope_done[(par, dh)] = c
                else:
                    b.w("scalar", "pe", c_mm)
                    b.w("scalar", f"s{2 + par}", vstg_free.get(par, 0))
                    i = b.A.activation(vstg[par][:, dh * 1024:(dh + 1) * 1024],
                                       paH, AF.Copy)
                    vevac_done[(dh,)] = b.sig(i, "act")
                if dh == 0 and tb + 1 < nblk:
                    p1_stage_ln(tb + 1)
            xhT_mm[par] = b.cnt["pe"]
            if part in ("q", "k"):
                for dh in range(2):
                    b.w("tensor", "dve", rope_done[(par, dh)])
                    b.w("tensor", f"s{par}", fmstg_free.get(par, 0))
                    b.w("tensor", "dve", pe_ptfree.get(dh, 0))
                    last = None
                    for j in range(8):
                        i = b.T.transpose(PT[dh][:, j * 128:(j + 1) * 128],
                                          qks[dh][:, j * 128:(j + 1) * 128],
                                          t["identsb"][:])
                        if i is not None:
                            last = i
                    cpe = b.sig(last, "pe")
                    qks_free[dh] = cpe
                    b.w("vector", "pe", cpe)
                    i = b.V.tensor_copy(
                        fmstg[par][:, dh * 1024:(dh + 1) * 1024], PT[dh][:])
                    pe_ptfree[dh] = b.sig(i, "dve")
                sp = t["qsp"] if part == "q" else t["ksp"]
                b.w("scalar", "dve", b.cnt["dve"])
                dst = _r(sp, "(h d) T -> d h T", d=128)[:, :, tb * 128:(tb + 1) * 128]
                srcv = _r(fmstg[par][:], "p (h d) -> p h d", h=16)
                fmstg_free[par] = dma3(b, "scalar", f"s{par}", dst, srcv, 4)
            else:
                b.w("scalar", "act", b.cnt["act"])
                vstg_free[par] = b.dma("scalar", f"s{2 + par}",
                                       t["vsp"][tb * 128:(tb + 1) * 128, :],
                                       vstg[par][:])
            if tb + 1 < nblk:
                p1_stage_tp(tb + 1)
            emit_spass_tiles(2)
        part_end_pe = b.cnt["pe"]
    emit_spass_tiles(len(sp_tiles))

    b.w("tensor", "gq", b.cnt["gq"])
    prev_bcast = 0
    theta_copied = 0
    for mi, (accc, base) in enumerate(((25, 16), (26, 19), (27, 22))):
        i = b.T.matmul(pb[0:1, 1024 + mi:1025 + mi], stats[:, accc:accc + 1],
                       t["ones_f"][:], start=True, stop=True)
        c = b.sig(i, "pe")
        b.w("scalar", "pe", c)
        b.w("scalar", "pe", prev_bcast)
        i = b.A.activation(stats[0:1, 28:29], pb[0:1, 1024 + mi:1025 + mi],
                           AF.Copy, scale=TH_SCALE)
        c = b.sig(i, "act")
        b.w("tensor", "act", c)
        b.w("tensor", "dve", theta_copied)
        i = b.T.matmul(pb[:, 1536 + mi:1537 + mi], t["onesr_f"][:],
                       stats[0:1, 28:29], start=True, stop=True)
        c = b.sig(i, "pe")
        prev_bcast = c
        b.w("vector", "pe", c)
        i = b.V.tensor_copy(stats[:, base:base + 1], pb[:, 1536 + mi:1537 + mi])
        c = b.sig(i, "dve")
        theta_copied = c
        b.w("vector", "dve", c)
        i = b.V.tensor_scalar(stats[:, base + 1:base + 2],
                              stats[:, base:base + 1], -1.0, None, ALU.mult)
        b.sig(i, "dve")
        i = b.V.tensor_scalar(stats[:, base + 2:base + 3],
                              stats[:, base:base + 1], -2.0, None, ALU.mult)
        c = b.sig(i, "dve")
        b.w("vector", "dve", c)

    b.barrier(bst)
    if MAX_PHASE < 2:
        return

    bf = Carve(BFA, t["BF_COLS"])
    fa = Carve(FAA, t["FA_COLS"])
    masksb = bf.take(NSLOT * 8 * 512)
    kwin = [bf.take(4096), bf.take(4096), bf.take(4096)]
    vwin = [bf.take(4096), bf.take(4096), bf.take(4096)]
    qwin = [bf.take(512), bf.take(512), bf.take(512)]
    PTr = [bf.take(512) for _ in range(4)]
    attstg = [bf.take(512), bf.take(512)]
    rstg = [bf.take(2048), bf.take(2048)]
    qtl = [fa.take(2048), fa.take(2048)]
    rb_sb = [fa.take(512), fa.take(512)]
    recro = [fa.take(512), fa.take(512)]

    mld = b.dma("sync", "d6", masksb[:], t["masks"])

    q_tiles = []
    for mat, rsp, thbase in (("w1T", "r1sp", 16), ("w3T", "r3sp", 19)):
        for pc in range(16):
            for f0, fw in ((0, 2048), (2048, 2048), (4096, HID - 4096)):
                q_tiles.append((mat, rsp, thbase, pc * 128, 128, f0, fw))
    for pc in range(NHC):
        hw = 128 if pc < 42 else HC_LAST
        q_tiles.append(("w2T", "w2rsp", 22, pc * 128, hw, 0, 2048))
    q_idx = [0]
    q_ld = {}
    q_store = {}
    q_v = {}

    def emit_quant(n):
        for _ in range(n):
            k = q_idx[0]
            if k >= len(q_tiles):
                return
            mat, rsp, thb, r0, rows, f0, fw = q_tiles[k]
            slot = k % 2
            b.w("sync", "dve", q_v.get(k - 2, 0))
            c_ld = b.dma("sync", f"d{2 + slot}", qtl[slot][0:rows, 0:fw],
                         t[mat][r0:r0 + rows, f0:f0 + fw])
            q_ld[k] = c_ld
            b.w("vector", f"d{2 + slot}", c_ld)
            b.w("vector", f"s{2 + slot}", q_store.get(k - 2, 0))
            i = b.V.tensor_scalar(sqscr2[slot][0:rows, 0:fw],
                                  qtl[slot][0:rows, 0:fw],
                                  stats[0:rows, thb:thb + 1], None, ALU.is_gt)
            c = b.sig(i, "dve")
            b.w("vector", "dve", c)
            i = b.V.scalar_tensor_tensor(rstg[slot][0:rows, 0:fw],
                                         qtl[slot][0:rows, 0:fw],
                                         stats[0:rows, thb + 1:thb + 2],
                                         sqscr2[slot][0:rows, 0:fw],
                                         ALU.is_lt, ALU.subtract)
            cv = b.sig(i, "dve")
            q_v[k] = cv
            b.w("scalar", "dve", cv)
            q_store[k] = b.dma("scalar", f"s{2 + slot}",
                               t[rsp][r0:r0 + rows, f0:f0 + fw],
                               rstg[slot][0:rows, 0:fw])
            q_idx[0] += 1

    sqscr2 = [fa.take(2048), fa.take(2048)]

    win_free = {}
    sbank_free = {}
    pt_free = {}
    ov_free = {}
    att_free = {}
    l_free_m = {}
    rbp_free = {}
    rb_sb_free = {}
    recro_free = {}
    win_ld = {}

    passes = [(sl, h) for sl in range(NSLOT) for h in range(H)]

    def issue_windows(p):
        sl, h = passes[p]
        KT = KTs[sl]
        ring = p % 3
        b.w("sync", "pe", win_free.get(p - 3, 0))
        wsem = ("d0", "d1", "d7")[p % 3]
        b.dma("sync", wsem, qwin[ring][:],
              t["qsp"][h * 128:(h + 1) * 128, sl * 512:(sl + 1) * 512])
        b.dma("sync", wsem, kwin[ring][:, 0:KT * 128],
              t["ksp"][h * 128:(h + 1) * 128, 0:KT * 128])
        c = dma3(b, "sync", wsem,
                 _r(vwin[ring][:, 0:KT * 128], "p (kt d) -> p kt d", kt=KT),
                 _r(t["vsp"][0:KT * 128, h * 128:(h + 1) * 128],
                    "(kt k) d -> k kt d", k=128), (KT + 7) // 8)
        win_ld[p] = c

    prev_tail = [None]
    issue_windows(0)
    issue_windows(1)
    for p, (sl, h) in enumerate(passes):
        KT = KTs[sl]
        ring = p % 3
        par = p % 2
        if p + 2 < len(passes):
            issue_windows(p + 2)
        b.w("tensor", ("d0", "d1", "d7")[p % 3], win_ld[p])
        lastOV = None
        lastL = None

        def emit_scores_pair(ktp):
            pr = (ktp // 2) % 2
            b.w("tensor", "act", sbank_free.get(pr, 0))
            for kt in (ktp, ktp + 1):
                i = b.T.matmul(
                    pa[:, (2 * pr + kt % 2) * 512:(2 * pr + kt % 2 + 1) * 512],
                    kwin[ring][:, kt * 128:(kt + 1) * 128],
                    qwin[ring][:], start=True, stop=True)
                c_s = b.sig(i, "pe")
            b.w("scalar", "pe", c_s)
            b.w("scalar", "pe", pt_free.get(pr, 0))
            if p == 0:
                b.w("scalar", "d6", mld)
            bc2 = t["biascsb"][:, sl * 32 + ktp: sl * 32 + ktp + 1]
            ptv = dataclasses.replace(
                PTr[2 * pr][:], ap=[PTr[2 * pr][:].ap[0], [1, 1024]])
            i = b.A.activation(ptv, pa[:, pr * 1024:(pr + 1) * 1024],
                               AF.Exp, bias=bc2, scale=SCL)
            c_e = b.sig(i, "act")
            sbank_free[pr] = c_e
            for kt in (ktp, ktp + 1):
                pt_ready[kt] = ("act", c_e)
                if kt >= KT - 8:
                    mi = sl * 8 + (kt - (KT - 8))
                    b.w("vector", "act", c_e)
                    ptr = 2 * pr + (kt % 2)
                    i = b.V.tensor_tensor(PTr[ptr][:], PTr[ptr][:],
                                          masksb[:, mi * 512:(mi + 1) * 512],
                                          ALU.mult)
                    c_m = b.sig(i, "dve")
                    pt_ready[kt] = ("dve", c_m)

        def emit_ovl_pair(ktp):
            nonlocal lastOV, lastL
            pr = (ktp // 2) % 2
            for kt in (ktp, ktp + 1):
                ptr = 2 * pr + (kt % 2)
                b.w("tensor", pt_ready[kt][0], pt_ready[kt][1])
                if kt == 0:
                    b.w("tensor", "dve", ov_free.get(par, 0))
                    b.w("tensor", "dve", l_free_m.get(par, 0))
                    b.w("tensor", "dve", rbp_free.get(par, 0))
                i = b.T.matmul(ovb, vwin[ring][:, kt * 128:(kt + 1) * 128],
                               PTr[ptr][:], start=(kt == 0),
                               stop=(kt == KT - 1))
                lastOV = b.sig(i, "pe")
                i = b.T.matmul(lq[0:1, :], t["ones_b"][:], PTr[ptr][:],
                               start=(kt == 0), stop=(kt == KT - 1))
                lastL = b.sig(i, "pe")
            pt_free[pr] = lastL

        lq = pb[:, 0:512] if par == 0 else pb[:, 1024:1536]
        rb = lq
        ovb = pb[:, 512:1024] if par == 0 else pb[:, 1536:2048]
        pt_ready = {}
        for ktp in range(0, KT, 2):
            emit_scores_pair(ktp)
            if ktp >= 2:
                emit_ovl_pair(ktp - 2)
            if ktp == 2 and prev_tail[0] is not None:
                prev_tail[0]()
                prev_tail[0] = None
        emit_ovl_pair(KT - 2)
        win_free[p] = lastL

        def make_tail(p, par, sl, h, lastOV, lastL, lq, rb, ovb):
            def tail():
                b.w("vector", "pe", lastL)
                b.w("vector", "pe", recro_free.get(par, 0))
                i = b.V.reciprocal(recro[par][0:1, :], lq[0:1, :])
                c = b.sig(i, "dve")
                l_free_m[par] = c
                b.w("vector", "dve", c)
                b.w("tensor", "dve", c)
                b.w("tensor", "dve", rbp_free.get(par, 0))
                i = b.T.matmul(rb, t["onesr_f"][:], recro[par][0:1, :],
                               start=True, stop=True)
                c = b.sig(i, "pe")
                recro_free[par] = c
                b.w("vector", "pe", c)
                b.w("vector", "dve", rb_sb_free.get(par, 0))
                i = b.V.tensor_copy(rb_sb[par][:], rb)
                rbp_free[par] = b.sig(i, "dve")
                b.w("vector", "dve", rbp_free[par])
                b.w("vector", "pe", lastOV)
                b.w("vector", f"s{par}", att_free.get(par, 0))
                i = b.V.tensor_tensor(attstg[par][:], ovb,
                                      rb_sb[par][:], ALU.mult)
                c = b.sig(i, "dve")
                rb_sb_free[par] = c
                ov_free[par] = c
                b.w("scalar", "dve", c)
                att_free[par] = b.dma("scalar", f"s{par}",
                                      t["asp"][h * 128:(h + 1) * 128,
                                               sl * 512:(sl + 1) * 512],
                                      attstg[par][:])
            return tail

        prev_tail[0] = make_tail(p, par, sl, h, lastOV, lastL, lq, rb, ovb)
        emit_quant(3)
    prev_tail[0]()
    emit_quant(len(q_tiles))

    b.barrier(bst)
    if MAX_PHASE < 3:
        return

    bf = Carve(BFA, t["BF_COLS"])
    fa = Carve(FAA, t["FA_COLS"])
    wosb = bf.take(16 * 2048)
    attT = [bf.take(2048), bf.take(2048)]
    sqscr = bf.take(2048)
    x2tok = bf.take(2048)
    x2stg = [bf.take(2048), bf.take(2048)]
    xown = [fa.take(2048), fa.take(2048)]
    tmpf = fa.take(2048)
    xpstg = [fa.take(2048), fa.take(2048)]

    wosb3 = _r(wosb, "p (ec f) -> p ec f", ec=16)
    wold = 0
    for c4 in range(4):
        wold = dma3(b, "gpsimd", "gd", wosb3[:, :, c4 * 512:(c4 + 1) * 512],
                    _r(t["woutT"][:, c4 * 512:(c4 + 1) * 512],
                       "(ec p) f -> p ec f", p=128), 2)

    att_ld = {}
    xown_ld = {}
    up_free = {}
    tp_free3 = {}
    attT_free = {}
    xown_free = {}
    x2stg_free = {}
    xp_free = {}
    pe_ptfree3 = {}
    x2tp_done = {}
    up_freeA = {}

    def p3_loads(q):
        par = q % 2
        b.w("sync", "pe", attT_free.get(q - 2, 0))
        att_ld[q] = dma3(b, "sync", f"d{par}",
                         _r(attT[par][:], "p (ec d) -> p ec d", ec=16),
                         _r(t["asp"], "(ec p) T -> p ec T", p=128)
                         [:, :, q * 128:(q + 1) * 128], 4)
        b.w("sync", "dve", xown_free.get(q - 2, 0))
        xown_ld[q] = b.dma("sync", f"d{2 + par}", xown[par][:],
                           t["x_own"][q * 128:(q + 1) * 128, :])

    p3_loads(0)
    for q in range(16):
        par = q % 2
        X = pa if par == 0 else pb
        if q + 1 < 16:
            p3_loads(q + 1)
        b.w("tensor", f"d{par}", att_ld[q])
        b.w("tensor", "gd", wold)
        b.w("tensor", "dve", up_free.get(par, 0))
        b.w("tensor", "act", up_freeA.get(par, 0))
        b.w("tensor", "dve", tp_free3.get(par, 0))
        last = None
        for fh in range(2):
            for ec in range(16):
                for f2 in range(2):
                    o0 = fh * 1024 + f2 * 512
                    i = b.T.matmul(X[:, o0:o0 + 512],
                                   attT[par][:, ec * 128:(ec + 1) * 128],
                                   wosb3[:, ec, o0:o0 + 512],
                                   start=(ec == 0), stop=(ec == 15))
                    if i is not None:
                        last = i
        c_mm = b.sig(last, "pe")
        attT_free[q] = c_mm
        if P3SUB < 2:
            continue
        b.w("scalar", "pe", c_mm)
        b.w("vector", "pe", c_mm)
        if P3SUB == 2 and LNMODE < 4:
            if LNMODE in (1, 3):
                i = b.V.reduce_sum(stats[:, 0:1], X[:, 0:2048], axis=AX.X)
                b.sig(i, "dve")
            if LNMODE in (2, 3):
                i = b.A.activation(sqscr[:], X[:, 0:2048], AF.Square,
                                   accum_out=stats[:, 1:2])
                b.sig(i, "act")
            up_free[par] = b.cnt["dve"]
            up_freeA[par] = b.cnt["act"]
            continue
        ln_stats_psum(b, t, X, sqscr, 0)
        if P3SUB < 3:
            up_free[par] = b.cnt["dve"]
            up_freeA[par] = b.cnt["act"]
            continue
        b.w("vector", f"d{2 + par}", xown_ld[q])
        i = b.V.tensor_scalar(tmpf[:], X[:, 0:2048], stats[:, 2:3],
                              stats[:, 6:7], ALU.subtract, ALU.mult)
        c = b.sig(i, "dve")
        up_free[par] = c
        b.w("vector", "dve", c)
        b.w("vector", f"s{par}", xp_free.get(par, 0))
        i = b.V.tensor_add(xpstg[par][:], tmpf[:], xown[par][:])
        cxp = b.sig(i, "dve")
        xown_free[q] = cxp
        b.w("scalar", "dve", cxp)
        b.w("vector", "dve", cxp)
        xp_free[par] = b.dma("scalar", f"s{par}",
                             t["x1sp"][q * 128:(q + 1) * 128, :], xpstg[par][:])
        if P3SUB < 4:
            continue
        ln_stats_sbuf(b, t, xpstg[par][:], sqscr, 8)
        b.w("vector", "pe", x2tp_done.get(q - 1, 0))
        i = b.V.tensor_scalar(x2tok[:], xpstg[par][:], stats[:, 10:11],
                              stats[:, 14:15], ALU.subtract, ALU.mult)
        cx2 = b.sig(i, "dve")
        if P3SUB < 5:
            continue
        PT3 = [X[:, 0:512].bitcast(BF16), X[:, 512:1024].bitcast(BF16)]
        b.w("tensor", "dve", cx2)
        b.w("tensor", f"s{2 + par}", x2stg_free.get(par, 0))
        lastpe = 0
        for g in range(2):
            b.w("tensor", "dve", pe_ptfree3.get((par, g), 0))
            last = None
            for j8 in range(8):
                j = g * 8 + j8
                i = b.T.transpose(PT3[g][:, j8 * 128:(j8 + 1) * 128],
                                  x2tok[:, j * 128:(j + 1) * 128],
                                  t["identsb"][:])
                if i is not None:
                    last = i
            c = b.sig(last, "pe")
            lastpe = c
            b.w("vector", "pe", c)
            i = b.V.tensor_copy(x2stg[par][:, g * 1024:(g + 1) * 1024],
                                PT3[g][:])
            pe_ptfree3[(par, g)] = b.sig(i, "dve")
        x2tp_done[q] = lastpe
        tp_free3[par] = b.cnt["dve"]
        b.w("scalar", "dve", tp_free3[par])
        x2stg_free[par] = dma3(
            b, "scalar", f"s{2 + par}",
            _r(t["x2sp"], "(ec p) T -> p ec T", p=128)[:, :, q * 128:(q + 1) * 128],
            _r(x2stg[par][:], "p (ec d) -> p ec d", ec=16), 4)

    b.barrier(bst)
    if MAX_PHASE < 4:
        return

    bf = Carve(BFA, t["BF_COLS"])
    fa = Carve(FAA, t["FA_COLS"])
    x2T = bf.take(16 * 2048)
    r1s = [bf.take(2048), bf.take(2048)]
    r3s = [bf.take(2048), bf.take(2048)]
    astg = [bf.take(2048), bf.take(2048)]
    gstg = [bf.take(2048), bf.take(2048)]

    x2T3 = _r(x2T, "p (ec T) -> p ec T", ec=16)
    xld = 0
    for c4 in range(4):
        xld = dma3(b, "sync", "d4", x2T3[:, :, c4 * 512:(c4 + 1) * 512],
                   _r(t["x2sp"][:, c4 * 512:(c4 + 1) * 512],
                      "(ec p) T -> p ec T", p=128), 2)

    r_ld = {}
    silu_done = {}
    g_done = {}
    g_store = {}
    h1_free = 0
    h3_free = 0

    def p4_loads(hc):
        hw = 128 if hc < 42 else HC_LAST
        ring = hc % 2
        b.w("sync", "pe", r_mm_done.get(hc - 2, 0))
        dma3(b, "sync", f"d{ring}",
             _r(r1s[ring][:, 0:16 * hw], "p (ec h) -> p ec h", ec=16),
             _r(t["r1sp"][:, hc * 128:hc * 128 + hw], "(ec p) h -> p ec h", p=128), 4)
        c = dma3(b, "sync", f"d{ring}",
                 _r(r3s[ring][:, 0:16 * hw], "p (ec h) -> p ec h", ec=16),
                 _r(t["r3sp"][:, hc * 128:hc * 128 + hw],
                    "(ec p) h -> p ec h", p=128), 4)
        r_ld[hc] = c

    r_mm_done = {}
    p4_loads(0)
    for hc in range(NHC):
        hw = 128 if hc < 42 else HC_LAST
        ring = hc % 2
        if hc + 1 < NHC:
            p4_loads(hc + 1)
        r13 = _r(r1s[ring][:, 0:16 * hw], "p (ec h) -> p ec h", ec=16)
        r33 = _r(r3s[ring][:, 0:16 * hw], "p (ec h) -> p ec h", ec=16)
        b.w("tensor", f"d{ring}", r_ld[hc])
        b.w("tensor", "d4", xld)
        b.w("tensor", "act", h1_free)
        last = None
        for ec in range(16):
            for tq in range(4):
                i = b.T.matmul(pa[0:hw, tq * 512:(tq + 1) * 512],
                               r13[:, ec, :], x2T3[:, ec, tq * 512:(tq + 1) * 512],
                               start=(ec == 0), stop=(ec == 15))
                if i is not None:
                    last = i
        c_h1 = b.sig(last, "pe")
        b.w("scalar", "pe", c_h1)
        b.w("scalar", "dve", g_done.get(ring, 0))
        i = b.A.activation(astg[ring][0:hw, :], pa[0:hw, :], AF.Silu,
                           scale=stats[0:hw, 18:19])
        c_si = b.sig(i, "act")
        h1_free = c_si
        silu_done[hc] = c_si
        b.w("tensor", "dve", h3_free)
        last = None
        for ec in range(16):
            for tq in range(4):
                i = b.T.matmul(pb[0:hw, tq * 512:(tq + 1) * 512],
                               r33[:, ec, :], x2T3[:, ec, tq * 512:(tq + 1) * 512],
                               start=(ec == 0), stop=(ec == 15))
                if i is not None:
                    last = i
        c_h3 = b.sig(last, "pe")
        r_mm_done[hc] = c_h3
        b.w("vector", "pe", c_h3)
        b.w("vector", "act", c_si)
        b.w("vector", f"s{ring}", g_store.get(ring, 0))
        i = b.V.scalar_tensor_tensor(gstg[ring][0:hw, :], pb[0:hw, :],
                                     stats[0:hw, 21:22], astg[ring][0:hw, :],
                                     ALU.mult, ALU.mult)
        c_g = b.sig(i, "dve")
        h3_free = c_g
        g_done[ring] = c_g
        b.w("scalar", "dve", c_g)
        g_store[ring] = b.dma("scalar", f"s{ring}",
                              t["gsp"][hc * 128:hc * 128 + hw, :],
                              gstg[ring][0:hw, :])

    b.barrier(bst)
    if MAX_PHASE < 5:
        return

    bf = Carve(BFA, t["BF_COLS"])
    fa = Carve(FAA, t["FA_COLS"])
    gwin = [bf.take(42 * 512), bf.take(42 * 512)]
    glast = [bf.take(512), bf.take(512)]
    w2s = [bf.take(1024) for _ in range(4)]
    sqscr = bf.take(2048)
    outstg = [fa.take(2048) for _ in range(4)]
    xrel = [fa.take(2048), fa.take(2048)]

    gwin3 = [_r(gwin[0], "p (hc t) -> p hc t", hc=42),
             _r(gwin[1], "p (hc t) -> p hc t", hc=42)]
    PS = [pa[:, 0:1024], pa[:, 1024:2048], pb[:, 0:1024], pb[:, 1024:2048]]

    w2_ld = {}
    w2slot_free = {}
    g_ld = {}
    gwin_free = {}
    ps_free = {}
    out_free = {}
    xrel_lastuse = {}
    y_store = {}

    def p5_gload(tg):
        gr = tg % 2
        b.w("sync", "pe", gwin_free.get(gr, 0))
        dma3(b, "sync", ("d4", "d5")[gr], gwin3[gr][:, :, :],
             _r(t["gsp"][0:5376, tg * 512:(tg + 1) * 512],
                "(hc p) t -> p hc t", p=128), 6)
        c = b.dma("sync", ("d4", "d5")[gr], glast[gr][0:HC_LAST, :],
                  t["gsp"][5376:5376 + HC_LAST, tg * 512:(tg + 1) * 512])
        g_ld[tg] = c

    def p5_w2load(tg, fh, hc):
        ring = hc % 4
        hw = 128 if hc < 42 else HC_LAST
        b.w("sync", "pe", w2slot_free.get(ring, 0))
        c = b.dma("sync", f"d{ring}", w2s[ring][0:hw, :],
                  t["w2rsp"][hc * 128:hc * 128 + hw, fh * 1024:(fh + 1) * 1024])
        w2_ld[(tg, fh, hc)] = c

    p5_gload(0)
    for tg in range(4):
        gr = tg % 2
        if tg + 1 < 4:
            p5_gload(tg + 1)
        for fh in range(2):
            p5_w2load(tg, fh, 0)
            for hc in range(NHC):
                hw = 128 if hc < 42 else HC_LAST
                ring = hc % 4
                if hc + 1 < NHC:
                    p5_w2load(tg, fh, hc + 1)
                b.w("tensor", f"d{hc % 4}", w2_ld[(tg, fh, hc)])
                if hc == 0:
                    b.w("tensor", ("d4", "d5")[gr], g_ld[tg])
                    for tb in range(4):
                        b.w("tensor", "dve", ps_free.get(tb, 0))
                for tb in range(4):
                    lhs = (gwin3[gr][:, hc, tb * 128:(tb + 1) * 128] if hc < 42
                           else glast[gr][0:HC_LAST, tb * 128:(tb + 1) * 128])
                    for f2 in range(2):
                        i = b.T.matmul(PS[tb][:, f2 * 512:(f2 + 1) * 512],
                                       lhs, w2s[ring][0:hw, f2 * 512:(f2 + 1) * 512],
                                       start=(hc == 0), stop=(hc == NHC - 1))
                c = b.sig(i, "pe")
                w2slot_free[ring] = c
                if hc == NHC - 1:
                    lastpe = c
            for tb in range(4):
                b.w("vector", "pe", lastpe)
                b.w("vector", "dve", out_free.get((tg - 1, tb), 0) if fh == 0 else 0)
                i = b.V.tensor_scalar(outstg[tb][:, fh * 1024:(fh + 1) * 1024],
                                      PS[tb], stats[:, 24:25], None, ALU.mult)
                ps_free[tb] = b.sig(i, "dve")
        gwin_free[gr] = lastpe
        for tb in range(4):
            par = tb % 2
            q0 = tg * 512 + tb * 128
            b.w("sync", f"s{par}", xrel_lastuse.get(par, 0))
            c_xr = b.dma("sync", f"d{6 + par}", xrel[par][:],
                         t["x1sp"][q0:q0 + 128, :])
            b.w("vector", "dve", ps_free[tb])
            b.w("scalar", "dve", ps_free[tb])
            ln_stats_sbuf(b, t, outstg[tb][:], sqscr, 0)
            i = b.V.tensor_scalar(outstg[tb][:], outstg[tb][:], stats[:, 2:3],
                                  stats[:, 6:7], ALU.subtract, ALU.mult)
            c = b.sig(i, "dve")
            b.w("vector", "dve", c)
            b.w("vector", f"d{6 + par}", c_xr)
            i = b.V.tensor_add(xrel[par][:], outstg[tb][:], xrel[par][:])
            c = b.sig(i, "dve")
            out_free[(tg, tb)] = c
            b.w("sync", "dve", c)
            y_store[(tg, tb)] = b.dma("sync", f"s{par}", t["y"][q0:q0 + 128, :],
                                      xrel[par][:])
            xrel_lastuse[par] = y_store[(tg, tb)]

    b.w("sync", "s0", b.cnt["s0"])
    b.w("sync", "s1", b.cnt["s1"])
    b.barrier(bst)


def ln_stats_psum(b, t, X, sqscr, sb):
    return _ln_stats(b, t, X[:, 0:2048], sqscr, sb, serialize=True)


def ln_stats_sbuf(b, t, xsrc, sqscr, sb):
    return _ln_stats(b, t, xsrc, sqscr, sb, serialize=False)


def _ln_stats(b, t, src, sqscr, sb, serialize):
    stats = t["stats"]
    i = b.A.activation(sqscr[:], src, AF.Square,
                       accum_out=stats[:, sb + 1:sb + 2])
    ca = b.sig(i, "act")
    if serialize:
        b.w("vector", "act", ca)
    i = b.V.reduce_sum(stats[:, sb + 0:sb + 1], src, axis=AX.X)
    c1 = b.sig(i, "dve")
    b.w("vector", "dve", c1)
    if not serialize:
        b.w("vector", "act", ca)
    i = b.V.tensor_scalar(stats[:, sb + 2:sb + 3], stats[:, sb + 0:sb + 1],
                          INV_E, None, ALU.mult)
    c = b.sig(i, "dve")
    b.w("vector", "dve", c)
    i = b.V.scalar_tensor_tensor(stats[:, sb + 3:sb + 4],
                                 stats[:, sb + 2:sb + 3], -1.0,
                                 stats[:, sb + 2:sb + 3], ALU.mult, ALU.mult)
    c = b.sig(i, "dve")
    b.w("vector", "dve", c)
    i = b.V.tensor_scalar(stats[:, sb + 4:sb + 5], stats[:, sb + 1:sb + 2],
                          INV_E, stats[:, sb + 3:sb + 4], ALU.mult, ALU.add)
    cvar = b.sig(i, "dve")
    b.w("scalar", "dve", cvar)
    i = b.A.activation(stats[:, sb + 5:sb + 6], stats[:, sb + 4:sb + 5],
                       AF.Ln, bias=EPS)
    c = b.sig(i, "act")
    b.w("scalar", "act", c)
    i = b.A.activation(stats[:, sb + 6:sb + 7], stats[:, sb + 5:sb + 6],
                       AF.Exp, scale=-0.5)
    crstd = b.sig(i, "act")
    b.w("vector", "act", crstd)
    return crstd



def build_program():
    nc = bass.Bass("TRN2", target_bir_lowering=False)
    t = {}
    t["x_tok"] = nc.dram_tensor("x_tok", [S, E], F32, kind="ExternalInput").ap()
    t["x_own"] = nc.dram_tensor("x_own", [TO, E], F32, kind="ExternalInput").ap()
    t["wqkvT"] = nc.dram_tensor("wqkvT", [E, 3 * E], F32, kind="ExternalInput").ap()
    t["woutT"] = nc.dram_tensor("woutT", [E, E], F32, kind="ExternalInput").ap()
    t["w1T"] = nc.dram_tensor("w1T", [E, HID], F32, kind="ExternalInput").ap()
    t["w3T"] = nc.dram_tensor("w3T", [E, HID], F32, kind="ExternalInput").ap()
    t["w2T"] = nc.dram_tensor("w2T", [HID, E], F32, kind="ExternalInput").ap()
    t["cos_t"] = nc.dram_tensor("cos_t", [S, DH], F32, kind="ExternalInput").ap()
    t["sinA_t"] = nc.dram_tensor("sinA_t", [S, DH], F32, kind="ExternalInput").ap()
    t["cos_o"] = nc.dram_tensor("cos_o", [TO, DH], F32, kind="ExternalInput").ap()
    t["sinA_o"] = nc.dram_tensor("sinA_o", [TO, DH], F32, kind="ExternalInput").ap()
    t["masks"] = nc.dram_tensor("masks", [128, NSLOT * 8 * 512], BF16,
                                kind="ExternalInput").ap()
    t["biasc"] = nc.dram_tensor("biasc", [128, 128], F32, kind="ExternalInput").ap()
    t["ident"] = nc.dram_tensor("ident", [128, 128], BF16, kind="ExternalInput").ap()
    t["y"] = nc.dram_tensor("y", [TO, E], F32, kind="ExternalOutput").ap()
    t["ksp"] = nc.dram_tensor("ksp", [E, S], BF16, kind="Internal").ap()
    t["vsp"] = nc.dram_tensor("vsp", [S, E], BF16, kind="Internal").ap()
    t["qsp"] = nc.dram_tensor("qsp", [E, TO], BF16, kind="Internal").ap()
    t["asp"] = nc.dram_tensor("asp", [E, TO], BF16, kind="Internal").ap()
    t["x1sp"] = nc.dram_tensor("x1sp", [TO, E], F32, kind="Internal").ap()
    t["x2sp"] = nc.dram_tensor("x2sp", [E, TO], BF16, kind="Internal").ap()
    t["r1sp"] = nc.dram_tensor("r1sp", [E, HID], BF16, kind="Internal").ap()
    t["r3sp"] = nc.dram_tensor("r3sp", [E, HID], BF16, kind="Internal").ap()
    t["w2rsp"] = nc.dram_tensor("w2rsp", [HID, E], BF16, kind="Internal").ap()
    t["gsp"] = nc.dram_tensor("gsp", [5376 + HC_LAST, TO], BF16, kind="Internal").ap()

    t["BF_COLS"] = 55296
    t["FA_COLS"] = 16384
    t["consts"] = {EPS: 0, -0.5: 1, SCL: 2, TH_SCALE: 3}

    with ExitStack() as ctx:
        En = ctx.enter_context
        t["BFA"] = En(nc.sbuf_tensor("BFA", [128, t["BF_COLS"]], BF16))
        t["FAA"] = En(nc.sbuf_tensor("FAA", [128, t["FA_COLS"]], F32))
        t["stats"] = En(nc.sbuf_tensor("stats", [128, 32], F32))
        t["cpool"] = En(nc.sbuf_tensor("cpool", [128, 8], F32))
        t["ones_b"] = En(nc.sbuf_tensor("ones_b", [128, 1], BF16))
        t["ones_f"] = En(nc.sbuf_tensor("ones_f", [128, 1], F32))
        t["onesr_f"] = En(nc.sbuf_tensor("onesr_f", [1, 128], F32))
        t["identsb"] = En(nc.sbuf_tensor("identsb", [128, 128], BF16))
        t["biascsb"] = En(nc.sbuf_tensor("biascsb", [128, 128], F32))
        t["pa"] = En(nc.psum_tensor("pa", [128, 2048], F32))
        t["pb"] = En(nc.psum_tensor("pb", [128, 2048], F32))
        sem_names = (["ld", "gd", "sd", "st", "pe", "dve", "act", "gq", "ph"]
                     + [f"d{i}" for i in range(8)] + [f"s{i}" for i in range(4)])
        sems = {n: En(nc.semaphore(n)) for n in sem_names}
        for val, idx in t["consts"].items():
            nc.const_aps.register(F32, val, t["cpool"][:, idx:idx + 1])
        blk = En(nc.Block())

        def run(eng_name):
            def body(engobj):
                b = Bld(nc, eng_name, engobj, sems)
                emit_program(b, t)
            return body

        blk.sync(run("sync"))
        blk.gpsimd(run("gpsimd"))
        blk.vector(run("vector"))
        blk.scalar(run("scalar"))
        blk.tensor(run("tensor"))
    return nc



_PERM = np.concatenate([np.arange(0, 128, 2), np.arange(1, 128, 2)])


def _host_tables():
    inv = (1.0 / (np.float32(10000.0) **
                  (np.arange(0, DH, 2, dtype=np.float32) / np.float32(DH))))
    inv = inv.astype(np.float32)
    pos = np.arange(S, dtype=np.float32)
    freqs = np.outer(pos, inv).astype(np.float32)
    ang = np.concatenate([freqs, freqs], axis=1)
    cos = np.cos(ang).astype(np.float32)[:, _PERM]
    sin = np.sin(ang).astype(np.float32)[:, _PERM]
    sinA = sin.copy()
    sinA[:, 0:64] *= -1.0
    return np.ascontiguousarray(cos), np.ascontiguousarray(sinA)


def _host_masks(parity):
    m = np.ones((NSLOT, 8, 128, QT), np.float32)
    for sl in range(NSLOT):
        K = KTs[sl]
        for i in range(8):
            kt = K - 8 + i
            if parity == 0:
                diag = (i >= 4)
            else:
                diag = (i < 4)
            if diag:
                off = (i - 4) * 128 if parity == 0 else i * 128
                k_idx = np.arange(128)[:, None] + off
                q_idx = np.arange(QT)[None, :]
                m[sl, i] = (k_idx <= q_idx).astype(np.float32)
    out = m.reshape(NSLOT * 8, 128, QT).transpose(1, 0, 2).reshape(128, NSLOT * 8 * QT)
    return np.ascontiguousarray(out.astype(ml_dtypes.bfloat16))


def _host_bias(parity):
    bc = np.full((128, 128), EXP_SHIFT, np.float32)
    if parity == 1:
        for sl in range(NSLOT):
            K = KTs[sl]
            for kt in range(K - 4, K):
                bc[:, sl * 32 + kt] = -88.0
    return bc


def kernel(L0_attn_x, L0_attn_Wqkv, L0_attn_Wout, L0_ffn_w1, L0_ffn_w3,
           L0_ffn_w2):
    x = np.ascontiguousarray(np.asarray(L0_attn_x, dtype=np.float32))
    Wqkv = np.asarray(L0_attn_Wqkv, dtype=np.float32)
    Wout = np.asarray(L0_attn_Wout, dtype=np.float32)
    w1 = np.asarray(L0_ffn_w1, dtype=np.float32)
    w3 = np.asarray(L0_ffn_w3, dtype=np.float32)
    w2 = np.asarray(L0_ffn_w2, dtype=np.float32)

    Wq3 = Wqkv.reshape(3, H, DH, E)[:, :, _PERM, :]
    wqkvT = np.ascontiguousarray(Wq3.reshape(3 * E, E).T)
    woutT = np.ascontiguousarray(
        Wout.T.reshape(H, DH, E)[:, _PERM, :].reshape(E, E))
    w1T = np.ascontiguousarray(w1.T)
    w3T = np.ascontiguousarray(w3.T)
    w2T = np.ascontiguousarray(w2.T)
    cos_t, sinA_t = _host_tables()
    ident = np.eye(128, dtype=ml_dtypes.bfloat16)

    own_rows = {}
    for r in (0, 1):
        blocks = [1, 3, 5, 7] if r == 0 else [0, 2, 4, 6]
        rows = np.concatenate([np.arange(a * QT, (a + 1) * QT) for a in blocks])
        own_rows[r] = rows

    in_maps = []
    for core in range(8):
        bidx, r = core // 2, core % 2
        rows = own_rows[r]
        in_maps.append(dict(
            x_tok=np.ascontiguousarray(x[bidx]),
            x_own=np.ascontiguousarray(x[bidx][rows]),
            wqkvT=wqkvT, woutT=woutT, w1T=w1T, w3T=w3T, w2T=w2T,
            cos_t=cos_t, sinA_t=sinA_t,
            cos_o=np.ascontiguousarray(cos_t[rows]),
            sinA_o=np.ascontiguousarray(sinA_t[rows]),
            masks=_host_masks(r), biasc=_host_bias(r), ident=ident,
        ))

    nc = build_program()
    res = run_bass_kernel_spmd(nc, in_maps, core_ids=list(range(8)))

    out = np.empty((B, S, E), np.float32)
    for core in range(8):
        bidx, r = core // 2, core % 2
        out[bidx][own_rows[r]] = np.asarray(res.results[core]["y"],
                                            dtype=np.float32)
    return out


if __name__ == "__main__":
    nc = build_program()
    print("built ok; instructions:", nc.n_instructions())


# version 12 (final)
# speedup vs baseline: 1.0413x; 1.0413x over previous
import dataclasses
from contextlib import ExitStack

import numpy as np
import ml_dtypes

import concourse.bass as bass
import concourse.mybir as mybir
from concourse.bass_utils import run_bass_kernel_spmd

F32 = mybir.dt.float32
BF16 = mybir.dt.bfloat16
AF = mybir.ActivationFunctionType
ALU = mybir.AluOpType
AX = mybir.AxisListType

B = 4
S = 4096
E = 2048
H = 16
DH = 128
HID = 5461
TO = 2048
NB_K = 32
NB_Q = 16
KTs = [8, 16, 24, 32]
NSLOT = 4
QT = 512
NW2 = float(HID * E)
SCL = float(np.float32(DH ** -0.5))
EXP_SHIFT = -25.0
EPS = 1e-5
INV_E = float(np.float32(1.0 / E))
TH_SCALE = float(np.float32(0.5 / NW2))
NHC = 43
HC_LAST = HID - 42 * 128

ENGS = ("sync", "gpsimd", "vector", "scalar", "tensor")
import os as _os
MAX_PHASE = int(_os.environ.get("BASSK_MAX_PHASE", "9"))
P3SUB = int(_os.environ.get("BASSK_P3SUB", "9"))
LNMODE = int(_os.environ.get("BASSK_LNMODE", "4"))


def _r(ap, pattern, **kw):
    return ap.rearrange(pattern, **kw)


def bcast_mid(ap2d, rep):
    return dataclasses.replace(ap2d, ap=[ap2d.ap[0], [0, rep], ap2d.ap[1]])


def half_view(ap2d, nh, half):
    return dataclasses.replace(
        ap2d,
        offset=ap2d.offset + half * 64,
        ap=[ap2d.ap[0], [128, nh], [1, 64]],
    )


def half_bcast(ap2d, nh, half):
    return dataclasses.replace(
        ap2d,
        offset=ap2d.offset + half * 64,
        ap=[ap2d.ap[0], [0, nh], [1, 64]],
    )



def dma3(b, qeng, sem, out3, in3, parts):
    mid = out3.shape[1]
    assert in3.shape[1] == mid
    step = (mid + parts - 1) // parts
    c = 0
    for s0 in range(0, mid, step):
        s1 = min(mid, s0 + step)
        c = b.dma(qeng, sem, out3[:, s0:s1, :], in3[:, s0:s1, :])
    return c


class Carve:
    def __init__(self, tensor, total):
        self.t = tensor
        self.off = 0
        self.total = total

    def take(self, cols):
        assert self.off + cols <= self.total, (self.off, cols, self.total)
        ap = self.t[:, self.off:self.off + cols]
        self.off += cols
        return ap


class Bld:

    def __init__(self, nc, eng, engobj, sems):
        self.nc = nc
        self.eng = eng
        self.engobj = engobj
        self.sems = sems
        self.cnt = {k: 0 for k in sems}

        class _Null:
            def __getattr__(self, name):
                return lambda *a, **k: None

        null = _Null()
        self.S = engobj if eng == "sync" else null
        self.G = engobj if eng == "gpsimd" else null
        self.V = engobj if eng == "vector" else null
        self.A = engobj if eng == "scalar" else null
        self.T = engobj if eng == "tensor" else null

    def on(self, e):
        return self.eng == e

    def sig(self, inst, sem, n=1):
        self.cnt[sem] += n
        if inst is not None:
            inst.then_inc(self.sems[sem], n)
        return self.cnt[sem]

    def w(self, eng, sem, val):
        if val <= 0:
            return
        assert val <= self.cnt[sem], (sem, val, self.cnt[sem])
        if self.eng == eng:
            self.engobj.wait_ge(self.sems[sem], val)

    def dma(self, qeng, sem, out, in_):
        inst = None
        if self.eng == qeng:
            inst = self.engobj.dma_start(out=out, in_=in_)
        return self.sig(inst, sem, 16)

    def barrier(self, state):
        for sem in ["ld", "st"] + [f"d{i}" for i in range(8)] + [f"s{i}" for i in range(4)]:
            self.w("sync", sem, self.cnt[sem])
        for sem in ["sd", "act"] + [f"s{i}" for i in range(4)]:
            self.w("scalar", sem, self.cnt[sem])
        self.w("gpsimd", "gd", self.cnt["gd"])
        self.w("gpsimd", "gq", self.cnt["gq"])
        self.w("vector", "dve", self.cnt["dve"])
        self.w("tensor", "pe", self.cnt["pe"])
        for e in ENGS:
            inst = self.engobj.nop() if self.eng == e else None
            self.sig(inst, "ph", 1)
        tgt = self.cnt["ph"]
        self.engobj.wait_ge(self.sems["ph"], tgt)



def emit_program(b, t):
    nc = b.nc
    stats = t["stats"]
    cpool = t["cpool"]
    pa, pb = t["pa"], t["pb"]
    BFA, FAA = t["BFA"], t["FAA"]
    bst = {}

    for val, idx in t["consts"].items():
        b.V.memset(cpool[:, idx:idx + 1], val)
    b.V.memset(t["ones_b"][:], 1.0)
    b.V.memset(t["onesr_f"][:], 1.0)
    i = b.V.memset(t["ones_f"][:], 1.0)
    b.sig(i, "dve")
    b.dma("sync", "ld", t["identsb"][:], t["ident"])
    b.dma("sync", "ld", t["biascsb"][:], t["biasc"])
    b.barrier(bst)
    if MAX_PHASE < 1:
        return

    bf = Carve(BFA, t["BF_COLS"])
    fa = Carve(FAA, t["FA_COLS"])
    wsb = bf.take(16 * 2048)
    xnb = [bf.take(2048), bf.take(2048)]
    xhT = [bf.take(2048), bf.take(2048)]
    sqscr = bf.take(2048)
    qks = [bf.take(1024), bf.take(1024)]
    fmstg = [bf.take(2048), bf.take(2048)]
    vstg = [bf.take(2048), bf.take(2048)]
    xin = [fa.take(2048), fa.take(2048)]
    ctile = [fa.take(128), fa.take(128)]
    stile = [fa.take(128), fa.take(128)]
    t1f = [fa.take(1024), fa.take(1024)]
    t2f = [fa.take(1024), fa.take(1024)]
    wtl = [fa.take(2048), fa.take(2048)]
    spscr = bf.take(2048)

    wsb3 = _r(wsb, "p (ec d) -> p ec d", ec=16)

    sp_tiles = []
    for mat, acc in (("w1T", 25), ("w3T", 26)):
        for pc in range(16):
            for f0, fw in ((0, 2048), (2048, 2048), (4096, HID - 4096)):
                sp_tiles.append((mat, pc * 128, f0, fw, acc))
    for pc in range(NHC):
        hw = 128 if pc < 42 else HC_LAST
        sp_tiles.append(("w2T", pc * 128, 0, 2048, 27))
    sp_idx = [0]
    acc_init = {25: False, 26: False, 27: False}
    sp_done = {}
    sp_add = {}

    def emit_spass_tiles(n):
        for _ in range(n):
            if sp_idx[0] >= len(sp_tiles):
                return
            k = sp_idx[0]
            mat, r0, f0, fw, acc = sp_tiles[k]
            rows = 128 if mat != "w2T" else min(128, HID - r0)
            slot = k % 2
            b.w("sync", "act", sp_done.get(k - 2, 0))
            c_ld = b.dma("sync", f"d{4 + slot}", wtl[slot][0:rows, 0:fw],
                         t[mat][r0:r0 + rows, f0:f0 + fw])
            b.w("scalar", f"d{4 + slot}", c_ld)
            b.w("scalar", "gq", sp_add.get(k - 1, 0))
            i = b.A.activation(spscr[0:rows, 0:fw], wtl[slot][0:rows, 0:fw],
                               AF.Abs, accum_out=stats[0:rows, 28:29])
            c = b.sig(i, "act")
            sp_done[k] = c
            b.w("gpsimd", "act", c)
            if not acc_init[acc]:
                i = b.G.tensor_scalar(stats[0:rows, acc:acc + 1],
                                      stats[0:rows, 28:29], 1.0, None, ALU.mult)
                acc_init[acc] = True
            else:
                i = b.G.tensor_add(stats[0:rows, acc:acc + 1],
                                   stats[0:rows, acc:acc + 1],
                                   stats[0:rows, 28:29])
            c = b.sig(i, "gq")
            b.w("gpsimd", "gq", c)
            sp_add[k] = c
            sp_idx[0] += 1


    part_end_pe = 0
    ln_ring = [0, 8]
    pe_ptfree = {}
    qk_spill_done = {}
    v_spill_done = {}
    xhT_done = {}
    rope_done = {}
    vevac_done = {}
    qks_free = {}
    fmstg_free = {}
    vstg_free = {}
    xin_free = {}
    mm_done = {}
    xnb_cons = {}
    xhT_mm = {}

    PT = [pb[:, 0:512].bitcast(BF16), pb[:, 512:1024].bitcast(BF16)]

    def ln_normalize(xsrc, dst_bf, par, rows=128):
        sb = ln_ring[par % 2]
        c_sq_prev = None
        i = b.V.reduce_sum(stats[:, sb + 0:sb + 1], xsrc, axis=AX.X)
        c1 = b.sig(i, "dve")
        i = b.A.activation(sqscr[:], xsrc, AF.Square,
                           accum_out=stats[:, sb + 1:sb + 2])
        ca = b.sig(i, "act")
        b.w("vector", "dve", c1)
        b.w("vector", "act", ca)
        i = b.V.tensor_scalar(stats[:, sb + 2:sb + 3], stats[:, sb + 0:sb + 1],
                              INV_E, None, ALU.mult)
        c = b.sig(i, "dve")
        b.w("vector", "dve", c)
        i = b.V.scalar_tensor_tensor(stats[:, sb + 3:sb + 4],
                                     stats[:, sb + 2:sb + 3], -1.0,
                                     stats[:, sb + 2:sb + 3],
                                     ALU.mult, ALU.mult)
        c = b.sig(i, "dve")
        b.w("vector", "dve", c)
        i = b.V.tensor_scalar(stats[:, sb + 4:sb + 5], stats[:, sb + 1:sb + 2],
                              INV_E, stats[:, sb + 3:sb + 4],
                              ALU.mult, ALU.add)
        cvar = b.sig(i, "dve")
        b.w("scalar", "dve", cvar)
        i = b.A.activation(stats[:, sb + 5:sb + 6], stats[:, sb + 4:sb + 5],
                           AF.Ln, bias=EPS)
        c = b.sig(i, "act")
        b.w("scalar", "act", c)
        i = b.A.activation(stats[:, sb + 6:sb + 7], stats[:, sb + 5:sb + 6],
                           AF.Exp, scale=-0.5)
        crstd = b.sig(i, "act")
        b.w("vector", "act", crstd)
        i = b.V.tensor_scalar(dst_bf, xsrc, stats[:, sb + 2:sb + 3],
                              stats[:, sb + 6:sb + 7], ALU.subtract, ALU.mult)
        return b.sig(i, "dve")

    def xhat_transposes(par):
        b.w("vector", "pe", xhT_mm.get(par, 0))
        for g in range(2):
            b.w("tensor", "dve", pe_ptfree.get(g, 0))
            last = None
            for j8 in range(8):
                j = g * 8 + j8
                i = b.T.transpose(PT[g][:, j8 * 128:(j8 + 1) * 128],
                                  xnb[par][:, j * 128:(j + 1) * 128],
                                  t["identsb"][:])
                if i is not None:
                    last = i
            c = b.sig(last, "pe")
            b.w("vector", "pe", c)
            i = b.V.tensor_copy(xhT[par][:, g * 1024:(g + 1) * 1024], PT[g][:])
            pe_ptfree[g] = b.sig(i, "dve")
        xhT_done[par] = b.cnt["dve"]

    parts = [("q", 0, NB_Q), ("k", 2048, NB_K), ("v", 4096, NB_K)]
    for pidx, (part, poff, nblk) in enumerate(parts):
        b.w("gpsimd", "pe", part_end_pe)
        wld = 0
        for c4 in range(4):
            wld = dma3(b, "gpsimd", "gd",
                       wsb3[:, :, c4 * 512:(c4 + 1) * 512],
                       _r(t["wqkvT"][:, poff + c4 * 512: poff + (c4 + 1) * 512],
                          "(ec p) d -> p ec d", p=128), 2)
        if part == "q":
            xsrc_t, cos_d, sin_d = t["x_own"], t["cos_o"], t["sinA_o"]
        else:
            xsrc_t, cos_d, sin_d = t["x_tok"], t["cos_t"], t["sinA_t"]
        xld_map = {}
        ccs_map = {}
        cnorm_map = {}

        def p1_loads(tb):
            par = tb % 2
            dcnt, acnt = xin_free.get(par, (0, 0))
            b.w("sync", "dve", dcnt)
            b.w("sync", "act", acnt)
            xld_map[tb] = b.dma("sync", f"d{par}", xin[par][:],
                                xsrc_t[tb * 128:(tb + 1) * 128, :])
            if part in ("q", "k"):
                b.w("sync", "dve", rope_done.get((par, 0), 0))
                b.w("sync", "dve", rope_done.get((par, 1), 0))
                b.dma("sync", f"d{2 + par}", ctile[par][:],
                      cos_d[tb * 128:(tb + 1) * 128, :])
                ccs_map[par] = b.dma("sync", f"d{2 + par}", stile[par][:],
                                     sin_d[tb * 128:(tb + 1) * 128, :])

        def p1_stage_ln(tb):
            par = tb % 2
            b.w("vector", f"d{par}", xld_map[tb])
            b.w("scalar", f"d{par}", xld_map[tb])
            b.w("vector", "pe", xnb_cons.get(par, 0))
            cnorm = ln_normalize(xin[par][:], xnb[par][:], tb)
            xin_free[par] = (cnorm, b.cnt["act"])
            cnorm_map[tb] = cnorm

        def p1_stage_tp(tb):
            par = tb % 2
            b.w("tensor", "dve", cnorm_map[tb])
            xhat_transposes(par)
            xnb_cons[par] = b.cnt["pe"]

        p1_loads(0)
        p1_stage_ln(0)
        p1_stage_tp(0)
        for tb in range(nblk):
            par = tb % 2
            if tb + 1 < nblk:
                p1_loads(tb + 1)
            c_cs = ccs_map.get(par, 0)
            for dh in range(2):
                b.w("tensor", "act", vevac_done.get((dh,), 0))
                b.w("tensor", "dve", rope_done.get(("t1", dh), 0))
                b.w("tensor", "gd", wld)
                b.w("tensor", "dve", xhT_done[par])
                last = None
                for ec in range(16):
                    i = b.T.matmul(pa[:, dh * 1024: dh * 1024 + 512],
                                   xhT[par][:, ec * 128:(ec + 1) * 128],
                                   wsb3[:, ec, dh * 1024: dh * 1024 + 512],
                                   start=(ec == 0), stop=(ec == 15))
                    i2 = b.T.matmul(pa[:, dh * 1024 + 512: dh * 1024 + 1024],
                                    xhT[par][:, ec * 128:(ec + 1) * 128],
                                    wsb3[:, ec, dh * 1024 + 512: dh * 1024 + 1024],
                                    start=(ec == 0), stop=(ec == 15))
                    last = i2 if i2 is not None else i
                c_mm = b.sig(last, "pe")
                mm_done[(par, dh)] = c_mm
                paH = pa[:, dh * 1024:(dh + 1) * 1024]
                if part in ("q", "k"):
                    b.w("vector", "pe", c_mm)
                    b.w("vector", f"d{2 + par}", c_cs)
                    pa3 = _r(paH, "p (h d) -> p h d", h=8)
                    t13 = _r(t1f[dh][:], "p (h d) -> p h d", h=8)
                    cosb = bcast_mid(ctile[par][:], 8)
                    i = b.V.tensor_tensor(t13, pa3, cosb, ALU.mult)
                    b.sig(i, "dve")
                    for hf in range(2):
                        ov = half_view(t2f[dh][:], 8, hf)
                        iv = half_view(paH, 8, 1 - hf)
                        sv = half_bcast(stile[par][:], 8, hf)
                        i = b.V.tensor_tensor(ov, iv, sv, ALU.mult)
                        b.sig(i, "dve")
                    b.w("vector", "dve", b.cnt["dve"])
                    b.w("vector", "pe", qks_free.get(dh, 0))
                    i = b.V.tensor_add(qks[dh][:], t1f[dh][:], t2f[dh][:])
                    c = b.sig(i, "dve")
                    rope_done[("t1", dh)] = c
                    rope_done[(par, dh)] = c
                else:
                    b.w("scalar", "pe", c_mm)
                    b.w("scalar", f"s{2 + par}", vstg_free.get(par, 0))
                    i = b.A.activation(vstg[par][:, dh * 1024:(dh + 1) * 1024],
                                       paH, AF.Copy)
                    vevac_done[(dh,)] = b.sig(i, "act")
                if dh == 0 and tb + 1 < nblk:
                    p1_stage_ln(tb + 1)
            xhT_mm[par] = b.cnt["pe"]
            if part in ("q", "k"):
                for dh in range(2):
                    b.w("tensor", "dve", rope_done[(par, dh)])
                    b.w("tensor", f"s{par}", fmstg_free.get(par, 0))
                    b.w("tensor", "dve", pe_ptfree.get(dh, 0))
                    last = None
                    for j in range(8):
                        i = b.T.transpose(PT[dh][:, j * 128:(j + 1) * 128],
                                          qks[dh][:, j * 128:(j + 1) * 128],
                                          t["identsb"][:])
                        if i is not None:
                            last = i
                    cpe = b.sig(last, "pe")
                    qks_free[dh] = cpe
                    b.w("vector", "pe", cpe)
                    i = b.V.tensor_copy(
                        fmstg[par][:, dh * 1024:(dh + 1) * 1024], PT[dh][:])
                    pe_ptfree[dh] = b.sig(i, "dve")
                sp = t["qsp"] if part == "q" else t["ksp"]
                b.w("scalar", "dve", b.cnt["dve"])
                dst = _r(sp, "(h d) T -> d h T", d=128)[:, :, tb * 128:(tb + 1) * 128]
                srcv = _r(fmstg[par][:], "p (h d) -> p h d", h=16)
                fmstg_free[par] = dma3(b, "scalar", f"s{par}", dst, srcv, 4)
            else:
                b.w("scalar", "act", b.cnt["act"])
                vstg_free[par] = b.dma("scalar", f"s{2 + par}",
                                       t["vsp"][tb * 128:(tb + 1) * 128, :],
                                       vstg[par][:])
            if tb + 1 < nblk:
                p1_stage_tp(tb + 1)
            emit_spass_tiles(2)
        part_end_pe = b.cnt["pe"]
    emit_spass_tiles(len(sp_tiles))

    b.w("tensor", "gq", b.cnt["gq"])
    prev_bcast = 0
    theta_copied = 0
    for mi, (accc, base) in enumerate(((25, 16), (26, 19), (27, 22))):
        i = b.T.matmul(pb[0:1, 1024 + mi:1025 + mi], stats[:, accc:accc + 1],
                       t["ones_f"][:], start=True, stop=True)
        c = b.sig(i, "pe")
        b.w("scalar", "pe", c)
        b.w("scalar", "pe", prev_bcast)
        i = b.A.activation(stats[0:1, 28:29], pb[0:1, 1024 + mi:1025 + mi],
                           AF.Copy, scale=TH_SCALE)
        c = b.sig(i, "act")
        b.w("tensor", "act", c)
        b.w("tensor", "dve", theta_copied)
        i = b.T.matmul(pb[:, 1536 + mi:1537 + mi], t["onesr_f"][:],
                       stats[0:1, 28:29], start=True, stop=True)
        c = b.sig(i, "pe")
        prev_bcast = c
        b.w("vector", "pe", c)
        i = b.V.tensor_copy(stats[:, base:base + 1], pb[:, 1536 + mi:1537 + mi])
        c = b.sig(i, "dve")
        theta_copied = c
        b.w("vector", "dve", c)
        i = b.V.tensor_scalar(stats[:, base + 1:base + 2],
                              stats[:, base:base + 1], -1.0, None, ALU.mult)
        b.sig(i, "dve")
        i = b.V.tensor_scalar(stats[:, base + 2:base + 3],
                              stats[:, base:base + 1], -2.0, None, ALU.mult)
        c = b.sig(i, "dve")
        b.w("vector", "dve", c)

    b.barrier(bst)
    if MAX_PHASE < 2:
        return

    bf = Carve(BFA, t["BF_COLS"])
    fa = Carve(FAA, t["FA_COLS"])
    masksb = bf.take(NSLOT * 8 * 512)
    kwin = [bf.take(4096), bf.take(4096), bf.take(4096)]
    vwin = [bf.take(4096), bf.take(4096), bf.take(4096)]
    qwin = [bf.take(512), bf.take(512), bf.take(512)]
    PTr = [bf.take(512) for _ in range(6)]
    attstg = [bf.take(512), bf.take(512)]
    rstg = [bf.take(2048), bf.take(2048)]
    qtl = [fa.take(2048), fa.take(2048)]
    rb_sb = [fa.take(512), fa.take(512)]
    recro = [fa.take(512), fa.take(512)]

    mld = b.dma("sync", "d6", masksb[:], t["masks"])

    q_tiles = []
    for mat, rsp, thbase in (("w1T", "r1sp", 16), ("w3T", "r3sp", 19)):
        for pc in range(16):
            for f0, fw in ((0, 2048), (2048, 2048), (4096, HID - 4096)):
                q_tiles.append((mat, rsp, thbase, pc * 128, 128, f0, fw))
    for pc in range(NHC):
        hw = 128 if pc < 42 else HC_LAST
        q_tiles.append(("w2T", "w2rsp", 22, pc * 128, hw, 0, 2048))
    q_idx = [0]
    q_ld = {}
    q_store = {}
    q_v = {}

    def emit_quant(n):
        for _ in range(n):
            k = q_idx[0]
            if k >= len(q_tiles):
                return
            mat, rsp, thb, r0, rows, f0, fw = q_tiles[k]
            slot = k % 2
            b.w("sync", "dve", q_v.get(k - 2, 0))
            c_ld = b.dma("sync", f"d{2 + slot}", qtl[slot][0:rows, 0:fw],
                         t[mat][r0:r0 + rows, f0:f0 + fw])
            q_ld[k] = c_ld
            b.w("vector", f"d{2 + slot}", c_ld)
            b.w("vector", f"s{2 + slot}", q_store.get(k - 2, 0))
            i = b.V.tensor_scalar(sqscr2[slot][0:rows, 0:fw],
                                  qtl[slot][0:rows, 0:fw],
                                  stats[0:rows, thb:thb + 1], None, ALU.is_gt)
            c = b.sig(i, "dve")
            b.w("vector", "dve", c)
            i = b.V.scalar_tensor_tensor(rstg[slot][0:rows, 0:fw],
                                         qtl[slot][0:rows, 0:fw],
                                         stats[0:rows, thb + 1:thb + 2],
                                         sqscr2[slot][0:rows, 0:fw],
                                         ALU.is_lt, ALU.subtract)
            cv = b.sig(i, "dve")
            q_v[k] = cv
            b.w("scalar", "dve", cv)
            q_store[k] = b.dma("scalar", f"s{2 + slot}",
                               t[rsp][r0:r0 + rows, f0:f0 + fw],
                               rstg[slot][0:rows, 0:fw])
            q_idx[0] += 1

    sqscr2 = [fa.take(2048), fa.take(2048)]

    win_free = {}
    sbank_free = {}
    pt_free = {}
    ov_free = {}
    att_free = {}
    l_free_m = {}
    rbp_free = {}
    rb_sb_free = {}
    recro_free = {}
    win_ld = {}

    passes = [(sl, h) for sl in range(NSLOT) for h in range(H)]

    def issue_windows(p):
        sl, h = passes[p]
        KT = KTs[sl]
        ring = p % 3
        b.w("sync", "pe", win_free.get(p - 3, 0))
        wsem = ("d0", "d1", "d7")[p % 3]
        b.dma("sync", wsem, qwin[ring][:],
              t["qsp"][h * 128:(h + 1) * 128, sl * 512:(sl + 1) * 512])
        b.dma("sync", wsem, kwin[ring][:, 0:KT * 128],
              t["ksp"][h * 128:(h + 1) * 128, 0:KT * 128])
        c = dma3(b, "sync", wsem,
                 _r(vwin[ring][:, 0:KT * 128], "p (kt d) -> p kt d", kt=KT),
                 _r(t["vsp"][0:KT * 128, h * 128:(h + 1) * 128],
                    "(kt k) d -> k kt d", k=128), (KT + 7) // 8)
        win_ld[p] = c

    prev_tail = [None]
    issue_windows(0)
    issue_windows(1)
    for p, (sl, h) in enumerate(passes):
        KT = KTs[sl]
        ring = p % 3
        par = p % 2
        if p + 2 < len(passes):
            issue_windows(p + 2)
        b.w("tensor", ("d0", "d1", "d7")[p % 3], win_ld[p])
        lastOV = None
        lastL = None

        def emit_scores_pair(ktp):
            pr = (ktp // 2) % 2
            ppr = (ktp // 2) % 3
            b.w("tensor", "act", sbank_free.get(pr, 0))
            for kt in (ktp, ktp + 1):
                i = b.T.matmul(
                    pa[:, (2 * pr + kt % 2) * 512:(2 * pr + kt % 2 + 1) * 512],
                    kwin[ring][:, kt * 128:(kt + 1) * 128],
                    qwin[ring][:], start=True, stop=True)
                c_s = b.sig(i, "pe")
            b.w("scalar", "pe", c_s)
            b.w("scalar", "pe", pt_free.get(ppr, 0))
            if p == 0:
                b.w("scalar", "d6", mld)
            bc2 = t["biascsb"][:, sl * 32 + ktp: sl * 32 + ktp + 1]
            ptv = dataclasses.replace(
                PTr[2 * ppr][:], ap=[PTr[2 * ppr][:].ap[0], [1, 1024]])
            i = b.A.activation(ptv, pa[:, pr * 1024:(pr + 1) * 1024],
                               AF.Exp, bias=bc2, scale=SCL)
            c_e = b.sig(i, "act")
            sbank_free[pr] = c_e
            for kt in (ktp, ktp + 1):
                pt_ready[kt] = ("act", c_e)
                if kt >= KT - 8:
                    mi = sl * 8 + (kt - (KT - 8))
                    b.w("vector", "act", c_e)
                    ptr = 2 * ppr + (kt % 2)
                    i = b.V.tensor_tensor(PTr[ptr][:], PTr[ptr][:],
                                          masksb[:, mi * 512:(mi + 1) * 512],
                                          ALU.mult)
                    c_m = b.sig(i, "dve")
                    pt_ready[kt] = ("dve", c_m)

        def emit_ovl_pair(ktp):
            nonlocal lastOV, lastL
            ppr = (ktp // 2) % 3
            for kt in (ktp, ktp + 1):
                ptr = 2 * ppr + (kt % 2)
                b.w("tensor", pt_ready[kt][0], pt_ready[kt][1])
                if kt == 0:
                    b.w("tensor", "dve", ov_free.get(par, 0))
                    b.w("tensor", "dve", l_free_m.get(par, 0))
                    b.w("tensor", "dve", rbp_free.get(par, 0))
                i = b.T.matmul(ovb, vwin[ring][:, kt * 128:(kt + 1) * 128],
                               PTr[ptr][:], start=(kt == 0),
                               stop=(kt == KT - 1))
                lastOV = b.sig(i, "pe")
                i = b.T.matmul(lq[0:1, :], t["ones_b"][:], PTr[ptr][:],
                               start=(kt == 0), stop=(kt == KT - 1))
                lastL = b.sig(i, "pe")
            pt_free[ppr] = lastL

        lq = pb[:, 0:512] if par == 0 else pb[:, 1024:1536]
        rb = lq
        ovb = pb[:, 512:1024] if par == 0 else pb[:, 1536:2048]
        pt_ready = {}
        for ktp in range(0, KT, 2):
            emit_scores_pair(ktp)
            if ktp >= 4:
                emit_ovl_pair(ktp - 4)
            if ktp == 2 and prev_tail[0] is not None:
                prev_tail[0]()
                prev_tail[0] = None
        emit_ovl_pair(KT - 4)
        emit_ovl_pair(KT - 2)
        win_free[p] = lastL

        def make_tail(p, par, sl, h, lastOV, lastL, lq, rb, ovb):
            def tail():
                b.w("vector", "pe", lastL)
                b.w("vector", "pe", recro_free.get(par, 0))
                i = b.V.reciprocal(recro[par][0:1, :], lq[0:1, :])
                c = b.sig(i, "dve")
                l_free_m[par] = c
                b.w("vector", "dve", c)
                b.w("tensor", "dve", c)
                b.w("tensor", "dve", rbp_free.get(par, 0))
                i = b.T.matmul(rb, t["onesr_f"][:], recro[par][0:1, :],
                               start=True, stop=True)
                c = b.sig(i, "pe")
                recro_free[par] = c
                b.w("vector", "pe", c)
                b.w("vector", "dve", rb_sb_free.get(par, 0))
                i = b.V.tensor_copy(rb_sb[par][:], rb)
                rbp_free[par] = b.sig(i, "dve")
                b.w("vector", "dve", rbp_free[par])
                b.w("vector", "pe", lastOV)
                b.w("vector", f"s{par}", att_free.get(par, 0))
                i = b.V.tensor_tensor(attstg[par][:], ovb,
                                      rb_sb[par][:], ALU.mult)
                c = b.sig(i, "dve")
                rb_sb_free[par] = c
                ov_free[par] = c
                b.w("scalar", "dve", c)
                att_free[par] = b.dma("scalar", f"s{par}",
                                      t["asp"][h * 128:(h + 1) * 128,
                                               sl * 512:(sl + 1) * 512],
                                      attstg[par][:])
            return tail

        prev_tail[0] = make_tail(p, par, sl, h, lastOV, lastL, lq, rb, ovb)
        emit_quant(3)
    prev_tail[0]()
    emit_quant(len(q_tiles))

    b.barrier(bst)
    if MAX_PHASE < 3:
        return

    bf = Carve(BFA, t["BF_COLS"])
    fa = Carve(FAA, t["FA_COLS"])
    wosb = bf.take(16 * 2048)
    attT = [bf.take(2048), bf.take(2048)]
    sqscr = bf.take(2048)
    x2tok = bf.take(2048)
    x2stg = [bf.take(2048), bf.take(2048)]
    xown = [fa.take(2048), fa.take(2048)]
    tmpf = fa.take(2048)
    xpstg = [fa.take(2048), fa.take(2048)]

    wosb3 = _r(wosb, "p (ec f) -> p ec f", ec=16)
    wold = 0
    for c4 in range(4):
        wold = dma3(b, "gpsimd", "gd", wosb3[:, :, c4 * 512:(c4 + 1) * 512],
                    _r(t["woutT"][:, c4 * 512:(c4 + 1) * 512],
                       "(ec p) f -> p ec f", p=128), 2)

    att_ld = {}
    xown_ld = {}
    up_free = {}
    tp_free3 = {}
    attT_free = {}
    xown_free = {}
    x2stg_free = {}
    xp_free = {}
    pe_ptfree3 = {}
    x2tp_done = {}
    up_freeA = {}

    def p3_loads(q):
        par = q % 2
        b.w("sync", "pe", attT_free.get(q - 2, 0))
        att_ld[q] = dma3(b, "sync", f"d{par}",
                         _r(attT[par][:], "p (ec d) -> p ec d", ec=16),
                         _r(t["asp"], "(ec p) T -> p ec T", p=128)
                         [:, :, q * 128:(q + 1) * 128], 4)
        b.w("sync", "dve", xown_free.get(q - 2, 0))
        xown_ld[q] = b.dma("sync", f"d{2 + par}", xown[par][:],
                           t["x_own"][q * 128:(q + 1) * 128, :])

    p3_loads(0)
    for q in range(16):
        par = q % 2
        X = pa if par == 0 else pb
        if q + 1 < 16:
            p3_loads(q + 1)
        b.w("tensor", f"d{par}", att_ld[q])
        b.w("tensor", "gd", wold)
        b.w("tensor", "dve", up_free.get(par, 0))
        b.w("tensor", "act", up_freeA.get(par, 0))
        b.w("tensor", "dve", tp_free3.get(par, 0))
        last = None
        for fh in range(2):
            for ec in range(16):
                for f2 in range(2):
                    o0 = fh * 1024 + f2 * 512
                    i = b.T.matmul(X[:, o0:o0 + 512],
                                   attT[par][:, ec * 128:(ec + 1) * 128],
                                   wosb3[:, ec, o0:o0 + 512],
                                   start=(ec == 0), stop=(ec == 15))
                    if i is not None:
                        last = i
        c_mm = b.sig(last, "pe")
        attT_free[q] = c_mm
        if P3SUB < 2:
            continue
        b.w("scalar", "pe", c_mm)
        b.w("vector", "pe", c_mm)
        if P3SUB == 2 and LNMODE < 4:
            if LNMODE in (1, 3):
                i = b.V.reduce_sum(stats[:, 0:1], X[:, 0:2048], axis=AX.X)
                b.sig(i, "dve")
            if LNMODE in (2, 3):
                i = b.A.activation(sqscr[:], X[:, 0:2048], AF.Square,
                                   accum_out=stats[:, 1:2])
                b.sig(i, "act")
            up_free[par] = b.cnt["dve"]
            up_freeA[par] = b.cnt["act"]
            continue
        ln_stats_psum(b, t, X, sqscr, 0)
        if P3SUB < 3:
            up_free[par] = b.cnt["dve"]
            up_freeA[par] = b.cnt["act"]
            continue
        b.w("vector", f"d{2 + par}", xown_ld[q])
        i = b.V.tensor_scalar(tmpf[:], X[:, 0:2048], stats[:, 2:3],
                              stats[:, 6:7], ALU.subtract, ALU.mult)
        c = b.sig(i, "dve")
        up_free[par] = c
        b.w("vector", "dve", c)
        b.w("vector", f"s{par}", xp_free.get(par, 0))
        i = b.V.tensor_add(xpstg[par][:], tmpf[:], xown[par][:])
        cxp = b.sig(i, "dve")
        xown_free[q] = cxp
        b.w("scalar", "dve", cxp)
        b.w("vector", "dve", cxp)
        xp_free[par] = b.dma("scalar", f"s{par}",
                             t["x1sp"][q * 128:(q + 1) * 128, :], xpstg[par][:])
        if P3SUB < 4:
            continue
        ln_stats_sbuf(b, t, xpstg[par][:], sqscr, 8)
        b.w("vector", "pe", x2tp_done.get(q - 1, 0))
        i = b.V.tensor_scalar(x2tok[:], xpstg[par][:], stats[:, 10:11],
                              stats[:, 14:15], ALU.subtract, ALU.mult)
        cx2 = b.sig(i, "dve")
        if P3SUB < 5:
            continue
        PT3 = [X[:, 0:512].bitcast(BF16), X[:, 512:1024].bitcast(BF16)]
        b.w("tensor", "dve", cx2)
        b.w("tensor", f"s{2 + par}", x2stg_free.get(par, 0))
        lastpe = 0
        for g in range(2):
            b.w("tensor", "dve", pe_ptfree3.get((par, g), 0))
            last = None
            for j8 in range(8):
                j = g * 8 + j8
                i = b.T.transpose(PT3[g][:, j8 * 128:(j8 + 1) * 128],
                                  x2tok[:, j * 128:(j + 1) * 128],
                                  t["identsb"][:])
                if i is not None:
                    last = i
            c = b.sig(last, "pe")
            lastpe = c
            b.w("vector", "pe", c)
            i = b.V.tensor_copy(x2stg[par][:, g * 1024:(g + 1) * 1024],
                                PT3[g][:])
            pe_ptfree3[(par, g)] = b.sig(i, "dve")
        x2tp_done[q] = lastpe
        tp_free3[par] = b.cnt["dve"]
        b.w("scalar", "dve", tp_free3[par])
        x2stg_free[par] = dma3(
            b, "scalar", f"s{2 + par}",
            _r(t["x2sp"], "(ec p) T -> p ec T", p=128)[:, :, q * 128:(q + 1) * 128],
            _r(x2stg[par][:], "p (ec d) -> p ec d", ec=16), 4)

    b.barrier(bst)
    if MAX_PHASE < 4:
        return

    bf = Carve(BFA, t["BF_COLS"])
    fa = Carve(FAA, t["FA_COLS"])
    x2T = bf.take(16 * 2048)
    r1s = [bf.take(2048), bf.take(2048)]
    r3s = [bf.take(2048), bf.take(2048)]
    astg = [bf.take(2048), bf.take(2048)]
    gstg = [bf.take(2048), bf.take(2048)]

    x2T3 = _r(x2T, "p (ec T) -> p ec T", ec=16)
    xld = 0
    for c4 in range(4):
        xld = dma3(b, "sync", "d4", x2T3[:, :, c4 * 512:(c4 + 1) * 512],
                   _r(t["x2sp"][:, c4 * 512:(c4 + 1) * 512],
                      "(ec p) T -> p ec T", p=128), 2)

    r_ld = {}
    silu_done = {}
    g_done = {}
    g_store = {}
    h1_free = 0
    h3_free = 0

    def p4_loads(hc):
        hw = 128 if hc < 42 else HC_LAST
        ring = hc % 2
        b.w("sync", "pe", r_mm_done.get(hc - 2, 0))
        dma3(b, "sync", f"d{ring}",
             _r(r1s[ring][:, 0:16 * hw], "p (ec h) -> p ec h", ec=16),
             _r(t["r1sp"][:, hc * 128:hc * 128 + hw], "(ec p) h -> p ec h", p=128), 4)
        c = dma3(b, "sync", f"d{ring}",
                 _r(r3s[ring][:, 0:16 * hw], "p (ec h) -> p ec h", ec=16),
                 _r(t["r3sp"][:, hc * 128:hc * 128 + hw],
                    "(ec p) h -> p ec h", p=128), 4)
        r_ld[hc] = c

    r_mm_done = {}
    p4_loads(0)
    for hc in range(NHC):
        hw = 128 if hc < 42 else HC_LAST
        ring = hc % 2
        if hc + 1 < NHC:
            p4_loads(hc + 1)
        r13 = _r(r1s[ring][:, 0:16 * hw], "p (ec h) -> p ec h", ec=16)
        r33 = _r(r3s[ring][:, 0:16 * hw], "p (ec h) -> p ec h", ec=16)
        b.w("tensor", f"d{ring}", r_ld[hc])
        b.w("tensor", "d4", xld)
        b.w("tensor", "act", h1_free)
        last = None
        for ec in range(16):
            for tq in range(4):
                i = b.T.matmul(pa[0:hw, tq * 512:(tq + 1) * 512],
                               r13[:, ec, :], x2T3[:, ec, tq * 512:(tq + 1) * 512],
                               start=(ec == 0), stop=(ec == 15))
                if i is not None:
                    last = i
        c_h1 = b.sig(last, "pe")
        b.w("scalar", "pe", c_h1)
        b.w("scalar", "dve", g_done.get(ring, 0))
        i = b.A.activation(astg[ring][0:hw, :], pa[0:hw, :], AF.Silu,
                           scale=stats[0:hw, 18:19])
        c_si = b.sig(i, "act")
        h1_free = c_si
        silu_done[hc] = c_si
        b.w("tensor", "dve", h3_free)
        last = None
        for ec in range(16):
            for tq in range(4):
                i = b.T.matmul(pb[0:hw, tq * 512:(tq + 1) * 512],
                               r33[:, ec, :], x2T3[:, ec, tq * 512:(tq + 1) * 512],
                               start=(ec == 0), stop=(ec == 15))
                if i is not None:
                    last = i
        c_h3 = b.sig(last, "pe")
        r_mm_done[hc] = c_h3
        b.w("vector", "pe", c_h3)
        b.w("vector", "act", c_si)
        b.w("vector", f"s{ring}", g_store.get(ring, 0))
        i = b.V.scalar_tensor_tensor(gstg[ring][0:hw, :], pb[0:hw, :],
                                     stats[0:hw, 21:22], astg[ring][0:hw, :],
                                     ALU.mult, ALU.mult)
        c_g = b.sig(i, "dve")
        h3_free = c_g
        g_done[ring] = c_g
        b.w("scalar", "dve", c_g)
        g_store[ring] = b.dma("scalar", f"s{ring}",
                              t["gsp"][hc * 128:hc * 128 + hw, :],
                              gstg[ring][0:hw, :])

    b.barrier(bst)
    if MAX_PHASE < 5:
        return

    bf = Carve(BFA, t["BF_COLS"])
    fa = Carve(FAA, t["FA_COLS"])
    gwin = [bf.take(42 * 512), bf.take(42 * 512)]
    glast = [bf.take(512), bf.take(512)]
    w2s = [bf.take(1024) for _ in range(4)]
    sqscr = bf.take(2048)
    outstg = [fa.take(2048) for _ in range(4)]
    xrel = [fa.take(2048), fa.take(2048)]

    gwin3 = [_r(gwin[0], "p (hc t) -> p hc t", hc=42),
             _r(gwin[1], "p (hc t) -> p hc t", hc=42)]
    PS = [pa[:, 0:1024], pa[:, 1024:2048], pb[:, 0:1024], pb[:, 1024:2048]]

    w2_ld = {}
    w2slot_free = {}
    g_ld = {}
    gwin_free = {}
    ps_free = {}
    out_free = {}
    xrel_lastuse = {}
    y_store = {}

    def p5_gload(tg):
        gr = tg % 2
        b.w("sync", "pe", gwin_free.get(gr, 0))
        dma3(b, "sync", ("d4", "d5")[gr], gwin3[gr][:, :, :],
             _r(t["gsp"][0:5376, tg * 512:(tg + 1) * 512],
                "(hc p) t -> p hc t", p=128), 6)
        c = b.dma("sync", ("d4", "d5")[gr], glast[gr][0:HC_LAST, :],
                  t["gsp"][5376:5376 + HC_LAST, tg * 512:(tg + 1) * 512])
        g_ld[tg] = c

    def p5_w2load(tg, fh, hc):
        ring = hc % 4
        hw = 128 if hc < 42 else HC_LAST
        b.w("sync", "pe", w2slot_free.get(ring, 0))
        c = b.dma("sync", f"d{ring}", w2s[ring][0:hw, :],
                  t["w2rsp"][hc * 128:hc * 128 + hw, fh * 1024:(fh + 1) * 1024])
        w2_ld[(tg, fh, hc)] = c

    p5_gload(0)
    for tg in range(4):
        gr = tg % 2
        if tg + 1 < 4:
            p5_gload(tg + 1)
        for fh in range(2):
            p5_w2load(tg, fh, 0)
            for hc in range(NHC):
                hw = 128 if hc < 42 else HC_LAST
                ring = hc % 4
                if hc + 1 < NHC:
                    p5_w2load(tg, fh, hc + 1)
                b.w("tensor", f"d{hc % 4}", w2_ld[(tg, fh, hc)])
                if hc == 0:
                    b.w("tensor", ("d4", "d5")[gr], g_ld[tg])
                    for tb in range(4):
                        b.w("tensor", "dve", ps_free.get(tb, 0))
                for tb in range(4):
                    lhs = (gwin3[gr][:, hc, tb * 128:(tb + 1) * 128] if hc < 42
                           else glast[gr][0:HC_LAST, tb * 128:(tb + 1) * 128])
                    for f2 in range(2):
                        i = b.T.matmul(PS[tb][:, f2 * 512:(f2 + 1) * 512],
                                       lhs, w2s[ring][0:hw, f2 * 512:(f2 + 1) * 512],
                                       start=(hc == 0), stop=(hc == NHC - 1))
                c = b.sig(i, "pe")
                w2slot_free[ring] = c
                if hc == NHC - 1:
                    lastpe = c
            for tb in range(4):
                b.w("vector", "pe", lastpe)
                b.w("vector", "dve", out_free.get((tg - 1, tb), 0) if fh == 0 else 0)
                i = b.V.tensor_scalar(outstg[tb][:, fh * 1024:(fh + 1) * 1024],
                                      PS[tb], stats[:, 24:25], None, ALU.mult)
                ps_free[tb] = b.sig(i, "dve")
        gwin_free[gr] = lastpe
        for tb in range(4):
            par = tb % 2
            q0 = tg * 512 + tb * 128
            b.w("sync", f"s{par}", xrel_lastuse.get(par, 0))
            c_xr = b.dma("sync", f"d{6 + par}", xrel[par][:],
                         t["x1sp"][q0:q0 + 128, :])
            b.w("vector", "dve", ps_free[tb])
            b.w("scalar", "dve", ps_free[tb])
            ln_stats_sbuf(b, t, outstg[tb][:], sqscr, 0)
            i = b.V.tensor_scalar(outstg[tb][:], outstg[tb][:], stats[:, 2:3],
                                  stats[:, 6:7], ALU.subtract, ALU.mult)
            c = b.sig(i, "dve")
            b.w("vector", "dve", c)
            b.w("vector", f"d{6 + par}", c_xr)
            i = b.V.tensor_add(xrel[par][:], outstg[tb][:], xrel[par][:])
            c = b.sig(i, "dve")
            out_free[(tg, tb)] = c
            b.w("sync", "dve", c)
            y_store[(tg, tb)] = b.dma("sync", f"s{par}", t["y"][q0:q0 + 128, :],
                                      xrel[par][:])
            xrel_lastuse[par] = y_store[(tg, tb)]

    b.w("sync", "s0", b.cnt["s0"])
    b.w("sync", "s1", b.cnt["s1"])
    b.barrier(bst)


def ln_stats_psum(b, t, X, sqscr, sb):
    return _ln_stats(b, t, X[:, 0:2048], sqscr, sb)


def ln_stats_sbuf(b, t, xsrc, sqscr, sb):
    return _ln_stats(b, t, xsrc, sqscr, sb)


def _ln_stats(b, t, src, sqscr, sb):
    stats = t["stats"]
    i = b.A.activation(sqscr[:], src, AF.Square,
                       accum_out=stats[:, sb + 1:sb + 2])
    ca = b.sig(i, "act")
    b.w("vector", "act", ca)
    i = b.V.reduce_sum(stats[:, sb + 0:sb + 1], src, axis=AX.X)
    c1 = b.sig(i, "dve")
    b.w("vector", "dve", c1)
    i = b.V.tensor_scalar(stats[:, sb + 2:sb + 3], stats[:, sb + 0:sb + 1],
                          INV_E, None, ALU.mult)
    c = b.sig(i, "dve")
    b.w("vector", "dve", c)
    i = b.V.scalar_tensor_tensor(stats[:, sb + 3:sb + 4],
                                 stats[:, sb + 2:sb + 3], -1.0,
                                 stats[:, sb + 2:sb + 3], ALU.mult, ALU.mult)
    c = b.sig(i, "dve")
    b.w("vector", "dve", c)
    i = b.V.tensor_scalar(stats[:, sb + 4:sb + 5], stats[:, sb + 1:sb + 2],
                          INV_E, stats[:, sb + 3:sb + 4], ALU.mult, ALU.add)
    cvar = b.sig(i, "dve")
    b.w("scalar", "dve", cvar)
    i = b.A.activation(stats[:, sb + 5:sb + 6], stats[:, sb + 4:sb + 5],
                       AF.Ln, bias=EPS)
    c = b.sig(i, "act")
    b.w("scalar", "act", c)
    i = b.A.activation(stats[:, sb + 6:sb + 7], stats[:, sb + 5:sb + 6],
                       AF.Exp, scale=-0.5)
    crstd = b.sig(i, "act")
    b.w("vector", "act", crstd)
    return crstd



def build_program():
    nc = bass.Bass("TRN2", target_bir_lowering=False)
    t = {}
    t["x_tok"] = nc.dram_tensor("x_tok", [S, E], F32, kind="ExternalInput").ap()
    t["x_own"] = nc.dram_tensor("x_own", [TO, E], F32, kind="ExternalInput").ap()
    t["wqkvT"] = nc.dram_tensor("wqkvT", [E, 3 * E], F32, kind="ExternalInput").ap()
    t["woutT"] = nc.dram_tensor("woutT", [E, E], F32, kind="ExternalInput").ap()
    t["w1T"] = nc.dram_tensor("w1T", [E, HID], F32, kind="ExternalInput").ap()
    t["w3T"] = nc.dram_tensor("w3T", [E, HID], F32, kind="ExternalInput").ap()
    t["w2T"] = nc.dram_tensor("w2T", [HID, E], F32, kind="ExternalInput").ap()
    t["cos_t"] = nc.dram_tensor("cos_t", [S, DH], F32, kind="ExternalInput").ap()
    t["sinA_t"] = nc.dram_tensor("sinA_t", [S, DH], F32, kind="ExternalInput").ap()
    t["cos_o"] = nc.dram_tensor("cos_o", [TO, DH], F32, kind="ExternalInput").ap()
    t["sinA_o"] = nc.dram_tensor("sinA_o", [TO, DH], F32, kind="ExternalInput").ap()
    t["masks"] = nc.dram_tensor("masks", [128, NSLOT * 8 * 512], BF16,
                                kind="ExternalInput").ap()
    t["biasc"] = nc.dram_tensor("biasc", [128, 128], F32, kind="ExternalInput").ap()
    t["ident"] = nc.dram_tensor("ident", [128, 128], BF16, kind="ExternalInput").ap()
    t["y"] = nc.dram_tensor("y", [TO, E], F32, kind="ExternalOutput").ap()
    t["ksp"] = nc.dram_tensor("ksp", [E, S], BF16, kind="Internal").ap()
    t["vsp"] = nc.dram_tensor("vsp", [S, E], BF16, kind="Internal").ap()
    t["qsp"] = nc.dram_tensor("qsp", [E, TO], BF16, kind="Internal").ap()
    t["asp"] = nc.dram_tensor("asp", [E, TO], BF16, kind="Internal").ap()
    t["x1sp"] = nc.dram_tensor("x1sp", [TO, E], F32, kind="Internal").ap()
    t["x2sp"] = nc.dram_tensor("x2sp", [E, TO], BF16, kind="Internal").ap()
    t["r1sp"] = nc.dram_tensor("r1sp", [E, HID], BF16, kind="Internal").ap()
    t["r3sp"] = nc.dram_tensor("r3sp", [E, HID], BF16, kind="Internal").ap()
    t["w2rsp"] = nc.dram_tensor("w2rsp", [HID, E], BF16, kind="Internal").ap()
    t["gsp"] = nc.dram_tensor("gsp", [5376 + HC_LAST, TO], BF16, kind="Internal").ap()

    t["BF_COLS"] = 55296
    t["FA_COLS"] = 16384
    t["consts"] = {EPS: 0, -0.5: 1, SCL: 2, TH_SCALE: 3}

    with ExitStack() as ctx:
        En = ctx.enter_context
        t["BFA"] = En(nc.sbuf_tensor("BFA", [128, t["BF_COLS"]], BF16))
        t["FAA"] = En(nc.sbuf_tensor("FAA", [128, t["FA_COLS"]], F32))
        t["stats"] = En(nc.sbuf_tensor("stats", [128, 32], F32))
        t["cpool"] = En(nc.sbuf_tensor("cpool", [128, 8], F32))
        t["ones_b"] = En(nc.sbuf_tensor("ones_b", [128, 1], BF16))
        t["ones_f"] = En(nc.sbuf_tensor("ones_f", [128, 1], F32))
        t["onesr_f"] = En(nc.sbuf_tensor("onesr_f", [1, 128], F32))
        t["identsb"] = En(nc.sbuf_tensor("identsb", [128, 128], BF16))
        t["biascsb"] = En(nc.sbuf_tensor("biascsb", [128, 128], F32))
        t["pa"] = En(nc.psum_tensor("pa", [128, 2048], F32))
        t["pb"] = En(nc.psum_tensor("pb", [128, 2048], F32))
        sem_names = (["ld", "gd", "sd", "st", "pe", "dve", "act", "gq", "ph"]
                     + [f"d{i}" for i in range(8)] + [f"s{i}" for i in range(4)])
        sems = {n: En(nc.semaphore(n)) for n in sem_names}
        for val, idx in t["consts"].items():
            nc.const_aps.register(F32, val, t["cpool"][:, idx:idx + 1])
        blk = En(nc.Block())

        def run(eng_name):
            def body(engobj):
                b = Bld(nc, eng_name, engobj, sems)
                emit_program(b, t)
            return body

        blk.sync(run("sync"))
        blk.gpsimd(run("gpsimd"))
        blk.vector(run("vector"))
        blk.scalar(run("scalar"))
        blk.tensor(run("tensor"))
    return nc



_PERM = np.concatenate([np.arange(0, 128, 2), np.arange(1, 128, 2)])


def _host_tables():
    inv = (1.0 / (np.float32(10000.0) **
                  (np.arange(0, DH, 2, dtype=np.float32) / np.float32(DH))))
    inv = inv.astype(np.float32)
    pos = np.arange(S, dtype=np.float32)
    freqs = np.outer(pos, inv).astype(np.float32)
    ang = np.concatenate([freqs, freqs], axis=1)
    cos = np.cos(ang).astype(np.float32)[:, _PERM]
    sin = np.sin(ang).astype(np.float32)[:, _PERM]
    sinA = sin.copy()
    sinA[:, 0:64] *= -1.0
    return np.ascontiguousarray(cos), np.ascontiguousarray(sinA)


def _host_masks(parity):
    m = np.ones((NSLOT, 8, 128, QT), np.float32)
    for sl in range(NSLOT):
        K = KTs[sl]
        for i in range(8):
            kt = K - 8 + i
            if parity == 0:
                diag = (i >= 4)
            else:
                diag = (i < 4)
            if diag:
                off = (i - 4) * 128 if parity == 0 else i * 128
                k_idx = np.arange(128)[:, None] + off
                q_idx = np.arange(QT)[None, :]
                m[sl, i] = (k_idx <= q_idx).astype(np.float32)
    out = m.reshape(NSLOT * 8, 128, QT).transpose(1, 0, 2).reshape(128, NSLOT * 8 * QT)
    return np.ascontiguousarray(out.astype(ml_dtypes.bfloat16))


def _host_bias(parity):
    bc = np.full((128, 128), EXP_SHIFT, np.float32)
    if parity == 1:
        for sl in range(NSLOT):
            K = KTs[sl]
            for kt in range(K - 4, K):
                bc[:, sl * 32 + kt] = -88.0
    return bc


def kernel(L0_attn_x, L0_attn_Wqkv, L0_attn_Wout, L0_ffn_w1, L0_ffn_w3,
           L0_ffn_w2):
    x = np.ascontiguousarray(np.asarray(L0_attn_x, dtype=np.float32))
    Wqkv = np.asarray(L0_attn_Wqkv, dtype=np.float32)
    Wout = np.asarray(L0_attn_Wout, dtype=np.float32)
    w1 = np.asarray(L0_ffn_w1, dtype=np.float32)
    w3 = np.asarray(L0_ffn_w3, dtype=np.float32)
    w2 = np.asarray(L0_ffn_w2, dtype=np.float32)

    Wq3 = Wqkv.reshape(3, H, DH, E)[:, :, _PERM, :]
    wqkvT = np.ascontiguousarray(Wq3.reshape(3 * E, E).T)
    woutT = np.ascontiguousarray(
        Wout.T.reshape(H, DH, E)[:, _PERM, :].reshape(E, E))
    w1T = np.ascontiguousarray(w1.T)
    w3T = np.ascontiguousarray(w3.T)
    w2T = np.ascontiguousarray(w2.T)
    cos_t, sinA_t = _host_tables()
    ident = np.eye(128, dtype=ml_dtypes.bfloat16)

    own_rows = {}
    for r in (0, 1):
        blocks = [1, 3, 5, 7] if r == 0 else [0, 2, 4, 6]
        rows = np.concatenate([np.arange(a * QT, (a + 1) * QT) for a in blocks])
        own_rows[r] = rows

    in_maps = []
    for core in range(8):
        bidx, r = core // 2, core % 2
        rows = own_rows[r]
        in_maps.append(dict(
            x_tok=np.ascontiguousarray(x[bidx]),
            x_own=np.ascontiguousarray(x[bidx][rows]),
            wqkvT=wqkvT, woutT=woutT, w1T=w1T, w3T=w3T, w2T=w2T,
            cos_t=cos_t, sinA_t=sinA_t,
            cos_o=np.ascontiguousarray(cos_t[rows]),
            sinA_o=np.ascontiguousarray(sinA_t[rows]),
            masks=_host_masks(r), biasc=_host_bias(r), ident=ident,
        ))

    nc = build_program()
    res = run_bass_kernel_spmd(nc, in_maps, core_ids=list(range(8)))

    out = np.empty((B, S, E), np.float32)
    for core in range(8):
        bidx, r = core // 2, core % 2
        out[bidx][own_rows[r]] = np.asarray(res.results[core]["y"],
                                            dtype=np.float32)
    return out


if __name__ == "__main__":
    nc = build_program()
    print("built ok; instructions:", nc.n_instructions())
